# version 19
# baseline (speedup 1.0000x reference)
import math
import os
from contextlib import ExitStack

import numpy as np

import concourse.bass as bass
import concourse.mybir as mybir
from concourse.bass import DynSlice
from concourse.expressions_rust import RuntimeValue, BinaryOp
from concourse.bass_utils import run_bass_kernel_spmd

F32 = mybir.dt.float32
F32R = mybir.dt.float32r
F8 = mybir.dt.float8e4
BF16 = mybir.dt.bfloat16
ALU = mybir.AluOpType
AF = mybir.ActivationFunctionType
AX = mybir.AxisListType

D = 2048
T = 1024
NCK = 16
H0, HD0 = 8, 256
H1, HD1 = 16, 128
M0, M1 = 16.0, 8.0

ENGS = ("sync", "tensor", "vector", "scalar", "gpsimd")
DBG = os.environ.get("KDBG", "")
KSTAGE = int(os.environ.get("KSTAGE", "5"))
KH = int(os.environ.get("KH", "0"))
KDUMP = os.environ.get("KDUMP", "")


class Sem:
    def __init__(self, name, step):
        self.name, self.step, self.c, self.h = name, step, 0, None


class Reg:
    __slots__ = ("w", "r")

    def __init__(self):
        self.w, self.r = [], []


NQ = 4


class Builder:
    def __init__(self):
        self.plan = {e: [] for e in ENGS}
        self.sems = {n: Sem(n, 1) for n in ("PE", "DV", "AC", "GP", "CC")}
        for q in ("LD", "GD", "GX"):
            for i in range(NQ):
                self.sems[q + str(i)] = Sem(q + str(i), 16)
        self.qcount = {"LD": 0, "GD": 0, "GX": 0}
        self.esem = {"sync": "LD", "tensor": "PE", "vector": "DV",
                     "scalar": "AC", "gpsimd": "GP"}
        self.lastw = {e: {} for e in ENGS}

    def _wait(self, eng, sem, val):
        if self.lastw[eng].get(sem.name, -1) >= val:
            return
        self.lastw[eng][sem.name] = val
        self.plan[eng].append(lambda E, s=sem, v=val: E.wait_ge(s.h, v))

    def op(self, eng, emit, reads=(), writes=(), sem=None):
        deps = []
        for r in reads:
            deps += r.w
        for w in writes:
            deps += w.w + w.r
        best = {}
        for (s, v) in deps:
            if best.get(s.name, (None, -1))[1] < v:
                best[s.name] = (s, v)
        for (s, v) in best.values():
            self._wait(eng, s, v)
        sname = sem or self.esem[eng]
        if sname in ("LD", "GD", "GX"):
            k = self.qcount[sname]
            self.qcount[sname] += 1
            s = self.sems[sname + str(k % NQ)]
            if k >= NQ:
                self._wait(eng, s, 16 * (k // NQ))
        else:
            s = self.sems[sname]
        s.c += s.step
        tok = (s, s.c)

        def f(E, emit=emit, s=s):
            emit(E, lambda ins: ins.then_inc(s.h, s.step))
        self.plan[eng].append(f)
        for r in reads:
            r.r.append(tok)
        for w in writes:
            w.w, w.r = [tok], []
        return tok


def zones(regs, c0, c1, zsz=1024):
    return [regs[i] for i in range(c0 // zsz, min(len(regs), (c1 + zsz - 1) // zsz))]


SLOTS = []


def slot_of(c0):
    if c0 not in SLOTS:
        SLOTS.append(c0)
    return SLOTS.index(c0)


NSLOT_MAX = 64

_cache = {}


def build_program():
    if "nc" in _cache:
        return _cache["nc"]
    nc = bass.Bass("TRN2", target_bir_lowering=False, debug=False)
    B = Builder()

    di = lambda n, sh: nc.dram_tensor(n, sh, F32, kind="ExternalInput").ap()
    x_in = di("x_in", [D, T])
    l0_qT = di("l0_qT", [H0, 128, NCK * 256])
    l0_kT = di("l0_kT", [1, 128, NCK * 256])
    l0_vT = di("l0_vT", [1, 128, NCK * 256])
    l0_oT = di("l0_oT", [8, 128, NCK * 256])
    l0_qn = di("l0_qn", [128, 2])
    l0_kn = di("l0_kn", [128, 2])
    l0_w2 = di("l0_w2", [8, 128, NCK * 256])
    l0_sc = di("l0_sc", [1, 1])
    l0_sc1 = di("l0_sc1", [1, 1])
    l1_qT = di("l1_qT", [H1, 128, NCK * 128])
    l1_kT = di("l1_kT", [H1, 128, NCK * 128])
    l1_vT = di("l1_vT", [4, 128, NCK * 512])
    l1_wo = di("l1_wo", [8, 128, NCK * 256])
    l1_bqr = di("l1_bqr", [1, D])
    l1_bkr = di("l1_bkr", [1, D])
    l1_bvr = di("l1_bvr", [1, D])
    l1_bo = di("l1_bo", [128, 16])
    l1_wT = di("l1_wT", [16, 128, NCK * 128])
    cosT = di("cosT", [128, T])
    sinT = di("sinT", [128, T])
    crowI = di("crowI", [1, NSLOT_MAX])
    identI = di("identI", [128, 128])

    out = nc.dram_tensor("o", [D, T], F32, kind="ExternalOutput").ap()

    qs = nc.dram_tensor("qs", [D, T], F32, kind="Internal").ap()
    shk0 = nc.dram_tensor("shk0", [256, 2048], F32, kind="Internal", addr_space="Shared").ap()
    shv0 = nc.dram_tensor("shv0", [2048, 256], F32, kind="Internal", addr_space="Shared").ap()
    shk1 = nc.dram_tensor("shk1", [2048, 2048], F32, kind="Internal", addr_space="Shared").ap()
    shv1 = nc.dram_tensor("shv1", [2048, 2048], F32, kind="Internal", addr_space="Shared").ap()
    k1scr = nc.dram_tensor("k1scr", [2048, T], F32, kind="Internal").ap()
    bar_in = nc.dram_tensor("bar_in", [1, 8], F32, kind="Internal").ap()
    dqd = nc.dram_tensor("dqd", [16, 1024], F32, kind="Internal").ap()
    rowscr = nc.dram_tensor("rowscr", [8, 1024], F32, kind="Internal").ap()
    lscr = nc.dram_tensor("lscr", [2, 128, 4], F32, kind="Internal").ap()
    tscr = nc.dram_tensor("tscr", [2, 1024], F32, kind="Internal").ap()
    bar_out = nc.dram_tensor("bar_out", [8, 8], F32, kind="Internal").ap()

    es = ExitStack()
    sbt = lambda n, sh, dt=F32: es.enter_context(nc.sbuf_tensor(n, sh, dt))
    xbuf = sbt("xbuf", [128, NCK * T])
    pool = sbt("pool", [128, NCK * T])
    f8a = sbt("f8a", [128, NCK * T], F8)
    f8b = sbt("f8b", [128, 4096], F8)
    cost = sbt("cost", [128, T])
    sint = sbt("sint", [128, T])
    identt = sbt("identt", [128, 128])
    qnw = sbt("qnw", [128, 2])
    knw = sbt("knw", [128, 2])
    brstage = sbt("brstage", [1, 512])
    bot = sbt("bot", [128, 16])
    statr = sbt("statr", [1, 5632])
    dqrows = sbt("dqrows", [1, 1024])
    Rbuf = sbt("Rbuf", [128, 3072])
    Rtmp = sbt("Rtmp", [128, 1024])
    rtm = sbt("rtm", [128, 16])
    lsum = sbt("lsum", [128, 64])
    wtm = sbt("wtm", [128, 32])
    wtm2 = sbt("wtm2", [16, 2])
    djp = sbt("djp", [128, 512])
    cbc = sbt("cbc", [128, 112])
    crow = sbt("crow", [1, NSLOT_MAX])
    b_m0 = sbt("b_m0", [128, 1])
    b_m1 = sbt("b_m1", [128, 1])
    b_e5 = sbt("b_e5", [1, 1])
    b_e6 = sbt("b_e6", [1, 1])
    b_e6k = sbt("b_e6k", [1, 1])
    ones_p = sbt("ones_p", [128, 1])
    ones_r = sbt("ones_r", [1, 128])
    scv = sbt("scv", [1, 2])
    srsc = sbt("srsc", [1, 1])
    f4b = sbt("f4b", [128, 1])
    sc1b = sbt("sc1b", [128, 1])
    obf = sbt("obf", [128, 1024], BF16)
    ps = [es.enter_context(nc.psum_tensor("ps%d" % i, [128, 512], F32)) for i in range(8)]

    RX = [Reg() for _ in range(NCK)]
    RP = [Reg() for _ in range(NCK)]
    RF8 = [Reg() for _ in range(4)]
    RF8B = [Reg() for _ in range(2)]
    RPS = [Reg() for _ in range(8)]
    RST = [Reg() for _ in range(6)]
    RRB = [Reg() for _ in range(4)]
    RRT = Reg()
    Rrtm, Rlsum, Rscv, Rf4, Robf = Reg(), Reg(), Reg(), Reg(), Reg()
    Rdq, Rsmall, Rcbc = Reg(), Reg(), Reg()
    Rqtm, Rwtm = Reg(), Reg()
    Rwtm2, Robf2, Rwc1, Rcbc2 = Reg(), Reg(), Reg(), Reg()
    Rrow = [Reg() for _ in range(8)]
    Rlscr = [Reg(), Reg()]
    Rtscr = [Reg(), Reg()]
    Rqs, Rshk0, Rshv0, Rshk1, Rshv1, Rbar = Reg(), Reg(), Reg(), Reg(), Reg(), Reg()
    Rk1s, Rdq2 = Reg(), Reg()
    Rbrs = Reg()

    def ld(dst_ap, src_ap, regs):
        return B.op("sync", lambda E, tk, d=dst_ap, s=src_ap: tk(E.dma_start(out=d, in_=s)),
                    reads=(), writes=regs, sem="LD")

    def vop(emit, reads=(), writes=()):
        return B.op("vector", emit, reads, writes)

    def aop(emit, reads=(), writes=()):
        return B.op("scalar", emit, reads, writes)

    def gop(emit, reads=(), writes=(), sem=None):
        return B.op("gpsimd", emit, reads, writes, sem=sem)

    def top(emit, reads=(), writes=()):
        return B.op("tensor", emit, reads, writes)

    _bslot = [0]

    def bcast_dma(dst_fn, dst_regs, row_fn, row_regs, width):
        sl = _bslot[0] % 8
        _bslot[0] += 1
        B.op("gpsimd", lambda E, tk, sl=sl: tk(E.dma_start(
            out=rowscr[sl:sl + 1, 0:width], in_=row_fn()[0:1, 0:width])),
            reads=list(row_regs), writes=[Rrow[sl]], sem="GD")
        B.op("gpsimd", lambda E, tk, sl=sl: tk(E.dma_start(
            out=dst_fn()[:, 0:width],
            in_=rowscr[sl:sl + 1, 0:width].partition_broadcast(128))),
            reads=[Rrow[sl]], writes=list(dst_regs), sem="GD")

    def bcastf(dst_fn, dst_regs, row_fn, row_regs, width, bank):
        for c0 in range(0, width, 512):
            w = min(512, width - c0)
            top(lambda E, tk, c0=c0, w=w: tk(E.matmul(
                ps[bank][:, 0:w], ones_r[:], row_fn()[0:1, c0:c0 + w],
                start=True, stop=True)),
                reads=list(row_regs) + [Rsmall], writes=[RPS[bank]])
            aop(lambda E, tk, c0=c0, w=w: tk(E.copy(
                dst_fn()[:, c0:c0 + w], ps[bank][:, 0:w])),
                reads=[RPS[bank]], writes=list(dst_regs))

    def bdump(row0, src_ap, nrows):
        for s in B.sems.values():
            if s.c > 0:
                B._wait("gpsimd", s, s.c)
        B.op("gpsimd", lambda E, tk, src_ap=src_ap, row0=row0, nrows=nrows: tk(
            E.dma_start(out=out[row0:row0 + nrows, 0:src_ap.shape[-1]], in_=src_ap)),
            reads=(), writes=(), sem="GD")

    def bdump1(row0, semname):
        B.op("gpsimd", lambda E, tk, row0=row0: tk(
            E.dma_start(out=out[row0:row0 + 1, 0:1024], in_=statr[0:1, 0:1024])),
            reads=(), writes=(), sem="GD")

    for dc in range(NCK):
        ld(xbuf[:, dc * T:(dc + 1) * T], x_in[dc * 128:(dc + 1) * 128, :], [RX[dc]])
    ld(cost[:], cosT[:], [Rsmall])
    ld(sint[:], sinT[:], [Rsmall])
    ld(identt[:], identI[:], [Rsmall])
    ld(qnw[:], l0_qn[:], [Rsmall])
    ld(knw[:], l0_kn[:], [Rsmall])
    ld(bot[:], l1_bo[:], [Rsmall])
    ld(scv[:, 0:1], l0_sc[:], [Rscv])
    ld(scv[:, 1:2], l0_sc1[:], [Rscv])
    ld(crow[:], crowI[:], [Rcbc])

    vop(lambda E, tk: tk(E.memset(b_m0[:], -M0)), writes=[Rsmall])
    vop(lambda E, tk: tk(E.memset(b_m1[:], -M1)), writes=[Rsmall])
    vop(lambda E, tk: tk(E.memset(b_e5[:], 1e-5)), writes=[Rsmall])
    vop(lambda E, tk: tk(E.memset(b_e6[:], 1e-6)), writes=[Rsmall])
    vop(lambda E, tk: tk(E.memset(b_e6k[:], 256e-6)), writes=[Rsmall])
    vop(lambda E, tk: tk(E.memset(ones_p[:], 1.0)), writes=[Rsmall])
    vop(lambda E, tk: tk(E.memset(ones_r[:], 1.0)), writes=[Rsmall])
    gop(lambda E, tk: tk(E.iota(djp[:], [[1, 512]], base=0, channel_multiplier=-1,
                                allow_small_or_imprecise_dtypes=True)), writes=[Rsmall])
    bcastf(lambda: cbc, [Rcbc], lambda: crow, [Rcbc], NSLOT_MAX, 7)

    S_MU, S_MUN, S_SG, S_RI, S_HD, S_WC = 0, 1, 2, 3, 4, 5
    S_SC = S_HD
    srow = lambda z: statr[0:1, z * 1024:(z + 1) * 1024]

    def layernorm_stats():
        for qi in range(2):
            for th in range(2):
                for dc in range(NCK):
                    sqo = (dc % 2) * 1024
                    if qi == 1:
                        aop(lambda E, tk, dc=dc, th=th, sqo=sqo: tk(E.activation(
                            pool[:, sqo:sqo + 512],
                            xbuf[:, dc * T + th * 512: dc * T + (th + 1) * 512],
                            AF.Square)), reads=[RX[dc]], writes=[RP[0]] if sqo == 0 else [RP[1]])
                        rreg = RP[0] if sqo == 0 else RP[1]
                        rap = (lambda dc=dc, th=th, sqo=sqo: pool[:, sqo:sqo + 512])
                    else:
                        rreg = RX[dc]
                        rap = (lambda dc=dc, th=th:
                               xbuf[:, dc * T + th * 512: dc * T + (th + 1) * 512])
                    top(lambda E, tk, dc=dc, rap=rap: tk(E.matmul(
                        ps[6][0:1, 0:512], ones_p[:], rap(),
                        start=(dc == 0), stop=(dc == NCK - 1))),
                        reads=[rreg, Rsmall], writes=[RPS[6]])
                dz = S_HD if qi == 0 else S_MUN
                dst = statr[0:1, dz * 1024 + th * 512: dz * 1024 + (th + 1) * 512]
                aop(lambda E, tk, dst=dst, dz=dz: tk(E.copy(dst, ps[6][0:1, 0:512])),
                    reads=[RPS[6]], writes=[RST[dz]])
        sx = statr[0:1, S_HD * 1024: S_HD * 1024 + 1024]
        sq = statr[0:1, S_MUN * 1024: S_MUN * 1024 + 1024]
        RSC = [RST[S_HD], RST[S_MUN]]
        vop(lambda E, tk: tk(E.tensor_scalar(srow(S_MU), sx, 1.0 / 2048, None, op0=ALU.mult)),
            reads=RSC, writes=[RST[S_MU]])
        vop(lambda E, tk: tk(E.tensor_scalar(srow(S_SG), sq, 1.0 / 2048, None, op0=ALU.mult)),
            reads=RSC, writes=[RST[S_SG]])
        vop(lambda E, tk: tk(E.tensor_tensor(srow(S_RI), srow(S_MU), srow(S_MU), op=ALU.mult)),
            reads=[RST[S_MU]], writes=[RST[S_RI]])
        vop(lambda E, tk: tk(E.tensor_tensor(srow(S_SG), srow(S_SG), srow(S_RI), op=ALU.subtract)),
            reads=[RST[S_RI]], writes=[RST[S_SG]])
        vop(lambda E, tk: tk(E.tensor_scalar(srow(S_MUN), srow(S_MU), -1.0, None, op0=ALU.mult)),
            reads=[RST[S_MU]], writes=[RST[S_MUN]])
        aop(lambda E, tk: tk(E.activation(srow(S_SG), srow(S_SG), AF.Sqrt, bias=b_e5[:])),
            reads=[Rsmall], writes=[RST[S_SG]])
        vop(lambda E, tk: tk(E.reciprocal(srow(S_RI), srow(S_SG))),
            reads=[RST[S_SG]], writes=[RST[S_RI]])
        bcastf(lambda: Rbuf[:, 0:1024], [RRB[0]], lambda: srow(S_RI), [RST[S_RI]], 1024, 6)
        bcastf(lambda: Rbuf[:, 1024:2048], [RRB[1]], lambda: srow(S_MUN), [RST[S_MUN]], 1024, 6)
        for i in range(8):
            top(lambda E, tk, i=i: tk(E.transpose(
                ps[7][:, i:i + 1],
                statr[0:1, S_RI * 1024 + i * 128:S_RI * 1024 + (i + 1) * 128],
                identt[0:1, 0:1])), reads=[RST[S_RI], Rsmall], writes=[RPS[7]])
        aop(lambda E, tk: tk(E.copy(rtm[:, 0:8], ps[7][:, 0:8])),
            reads=[RPS[7]], writes=[Rrtm])

    def wcolsum(slab_c0, ncols, fcs, neg_sign):
        wz = zones(RP, slab_c0, slab_c0 + NCK * ncols)
        sgn = -1.0 if neg_sign else 1.0
        for ppi in range((fcs + 511) // 512):
            c0, c1 = ppi * 512, min(fcs, ppi * 512 + 512)
            for dc in range(NCK):
                top(lambda E, tk, dc=dc, c0=c0, c1=c1: tk(E.matmul(
                    ps[6][0:1, 0:c1 - c0], ones_p[:],
                    pool[:, slab_c0 + dc * ncols + c0: slab_c0 + dc * ncols + c1],
                    start=(dc == 0), stop=(dc == NCK - 1))),
                    reads=wz + [Rsmall], writes=[RPS[6]])
            vop(lambda E, tk, c0=c0, c1=c1: tk(E.tensor_scalar(
                statr[0:1, S_WC * 1024 + c0: S_WC * 1024 + c1],
                ps[6][0:1, 0:c1 - c0], sgn, None, op0=ALU.mult)),
                reads=[RPS[6]], writes=[RST[S_WC]])

    wcs = lambda c0, c1: statr[0:1, S_WC * 1024 + c0: S_WC * 1024 + c1]
    RWC = [RST[S_WC]]

    def rope6(base, rA_c, rB_c, rC_c, tqn):
        d0 = lambda: pool[:, base:base + tqn]
        d1 = lambda: pool[:, base + tqn:base + 2 * tqn]
        tA = lambda: pool[:, rA_c:rA_c + tqn]
        tB = lambda: pool[:, rB_c:rB_c + tqn]
        tC = lambda: pool[:, rC_c:rC_c + tqn]
        r0 = zones(RP, base, base + tqn)
        r1 = zones(RP, base + tqn, base + 2 * tqn)
        rA = zones(RP, rA_c, rA_c + tqn)
        rB = zones(RP, rB_c, rB_c + tqn)
        rC = zones(RP, rC_c, rC_c + tqn)
        c = lambda: cost[:, 0:tqn]
        s = lambda: sint[:, 0:tqn]
        vop(lambda E, tk: tk(E.tensor_tensor(tA(), c(), d1(), op=ALU.mult)),
            reads=r1 + [Rsmall], writes=rA)
        vop(lambda E, tk: tk(E.tensor_tensor(tB(), s(), d1(), op=ALU.mult)),
            reads=r1 + [Rsmall], writes=rB)
        vop(lambda E, tk: tk(E.tensor_tensor(tC(), s(), d0(), op=ALU.mult)),
            reads=r0 + [Rsmall], writes=rC)
        vop(lambda E, tk: tk(E.tensor_tensor(d1(), tC(), tA(), op=ALU.add)),
            reads=rA + rC, writes=r1)
        vop(lambda E, tk: tk(E.tensor_tensor(tC(), c(), d0(), op=ALU.mult)),
            reads=r0 + [Rsmall], writes=rC)
        vop(lambda E, tk: tk(E.tensor_tensor(d0(), tC(), tB(), op=ALU.subtract)),
            reads=rB + rC, writes=r0)

    WS0, WS1 = 0, 4096
    KST, VST = 8192, 10240
    QH, TMPA, TMPB = 12288, 14336, 15360

    _srv_cache = {}

    def get_srv(E, mulv):
        if mulv not in _srv_cache:
            if "pid" not in _srv_cache:
                pid = E.partition_id()
                _srv_cache["pid"] = RuntimeValue(
                    BinaryOp(pid, RuntimeValue(1), ALU.bitwise_and), 0, 1)
            _srv_cache[mulv] = RuntimeValue(
                BinaryOp(_srv_cache["pid"], RuntimeValue(mulv), ALU.mult), 0, mulv)
        return _srv_cache[mulv]

    def get_qb(E, g):
        if g == 0:
            return get_srv(E, 512)
        key = ("qb", 1)
        if key not in _srv_cache:
            _srv_cache[key] = RuntimeValue(
                BinaryOp(get_srv(E, 512), RuntimeValue(1024), ALU.add), 1024, 1536)
        return _srv_cache[key]

    def head_rms(base, eps_bias, extra_scale_pow2):
        for ch in range(2):
            aop(lambda E, tk, ch=ch: tk(E.activation(
                pool[:, TMPA:TMPA + 1024], pool[:, base + ch * 1024:base + (ch + 1) * 1024],
                AF.Square)), reads=zones(RP, base + ch * 1024, base + (ch + 1) * 1024),
                writes=zones(RP, TMPA, TMPA + 1024))
            for th in range(2):
                bank = 6 if th == 0 else 7
                top(lambda E, tk, th=th, ch=ch, bank=bank: tk(E.matmul(
                    ps[bank][0:1, 0:512], ones_p[:],
                    pool[:, TMPA + th * 512:TMPA + (th + 1) * 512],
                    start=(ch == 0), stop=(ch == 1))),
                    reads=zones(RP, TMPA, TMPA + 1024) + [Rsmall], writes=[RPS[bank]])
        for th in range(2):
            bank = 6 if th == 0 else 7
            aop(lambda E, tk, th=th, bank=bank: tk(E.copy(
                statr[0:1, S_HD * 1024 + th * 512:S_HD * 1024 + (th + 1) * 512],
                ps[bank][0:1, 0:512])), reads=[RPS[bank]], writes=[RST[S_HD]])
        aop(lambda E, tk: tk(E.activation(srow(S_HD), srow(S_HD), AF.Sqrt,
                                          bias=eps_bias[:], scale=extra_scale_pow2)),
            reads=[Rsmall], writes=[RST[S_HD]])
        vop(lambda E, tk: tk(E.reciprocal(srow(S_HD), srow(S_HD))), writes=[RST[S_HD]])

    def bcast_hd_mul(base, nchunks, width=1024):
        bcastf(lambda: Rtmp[:, 0:width], [RRT], lambda: srow(S_HD)[:, 0:width],
               [RST[S_HD]], width, 6)
        for ch in range(nchunks):
            vop(lambda E, tk, ch=ch: tk(E.tensor_tensor(
                pool[:, base + ch * width:base + (ch + 1) * width],
                pool[:, base + ch * width:base + (ch + 1) * width],
                Rtmp[:, 0:width], op=ALU.mult)),
                reads=[RRT], writes=zones(RP, base + ch * width, base + (ch + 1) * width))

    if KSTAGE >= 2:
        layernorm_stats()
        if KDUMP == 'sent':
            bdump(1025, statr[0:1, 0:1024], 1)
        if KDUMP == "st":
            for zz in range(5):
                B.op("gpsimd", lambda E, tk, zz=zz: tk(E.dma_start(
                    out=out[zz:zz + 1, :], in_=statr[0:1, zz * 1024:(zz + 1) * 1024])),
                    reads=[RST[i] for i in range(6)], writes=(), sem="GD")
            for rr in range(3):
                B.op("gpsimd", lambda E, tk, rr=rr: tk(E.dma_start(
                    out=out[128 * (rr + 1):128 * (rr + 2), :],
                    in_=Rbuf[:, rr * 1024:(rr + 1) * 1024])),
                    reads=[RRB[0], RRB[1], RRB[2]], writes=(), sem="GD")
            B.op("gpsimd", lambda E, tk: tk(E.dma_start(
                out=out[520:521, :], in_=rtm[:, 0:8].transpose() if False else statr[0:1, 0:1024])),
                reads=[Rrtm], writes=(), sem="GD")

        ld(pool[:, WS0:WS0 + 4096], l0_kT[0], zones(RP, WS0, WS0 + 4096))
        wcolsum(WS0, 256, 256, True)
        for ch in range(2):
            for th in range(2):
                bank = ch * 2 + th
                def emit(E, tk, ch=ch, th=th, bank=bank):
                    for dc in range(NCK):
                        E.matmul(ps[bank][:, 0:512],
                                 pool[:, WS0 + dc * 256 + ch * 128: WS0 + dc * 256 + ch * 128 + 128],
                                 xbuf[:, dc * T + th * 512: dc * T + (th + 1) * 512],
                                 start=(dc == 0), stop=False)
                    tk(E.matmul(ps[bank][:, 0:512],
                                wcs(ch * 128, ch * 128 + 128),
                                srow(S_MU)[:, th * 512:(th + 1) * 512],
                                start=False, stop=True))
                top(emit, reads=[RX[dc] for dc in range(NCK)] + zones(RP, WS0, WS0 + 4096)
                    + RWC + [RST[S_MU]], writes=[RPS[bank]])
                vop(lambda E, tk, b=bank, ch=ch, th=th: tk(E.tensor_tensor(
                    pool[:, KST + ch * 1024 + th * 512: KST + ch * 1024 + (th + 1) * 512],
                    ps[b][:, 0:512], Rbuf[:, th * 512:(th + 1) * 512], op=ALU.mult)),
                    reads=[RPS[bank], RRB[0]],
                    writes=zones(RP, KST + ch * 1024 + th * 512, KST + ch * 1024 + (th + 1) * 512))
        head_rms(KST, b_e6k, 1.0)
        for ch in range(2):
            vop(lambda E, tk, ch=ch: tk(E.tensor_scalar(
                pool[:, KST + ch * 1024:KST + (ch + 1) * 1024],
                pool[:, KST + ch * 1024:KST + (ch + 1) * 1024],
                knw[:, ch:ch + 1], None, op0=ALU.mult)),
                reads=[Rsmall], writes=zones(RP, KST + ch * 1024, KST + (ch + 1) * 1024))
        rope6(KST, VST, TMPA, TMPB, 1024)
        bcast_hd_mul(KST, 2)
        for sg in range(2):
            for sch in range(2):
                def store_shk0(E, tk, sg=sg, sch=sch):
                    rv = get_qb(E, sg)
                    c0 = KST + sch * 1024 + sg * 512
                    tk(E.dma_start(
                        out=shk0[sch * 128:(sch + 1) * 128, DynSlice(rv, 512)],
                        in_=pool[:, c0:c0 + 512]))
                gop(store_shk0, reads=zones(RP, KST + sch * 1024 + sg * 512,
                                            KST + sch * 1024 + sg * 512 + 512),
                    writes=[Rshk0], sem="GX")
        if KDUMP == 'sent':
            bdump(1026, statr[0:1, 0:1024], 1)
        if KDUMP == "kv":
            B.op("gpsimd", lambda E, tk: tk(E.dma_start(
                out=out[0:256, :].rearrange("(c p) t -> p c t", p=128),
                in_=pool[:, KST:KST + 2048].rearrange("p (c t) -> p c t", c=2))),
                reads=zones(RP, KST, KST + 2048), writes=(), sem="GD")

        ld(pool[:, WS1:WS1 + 4096], l0_vT[0], zones(RP, WS1, WS1 + 4096))
        wcolsum(WS1 , 256, 256, True)
        for tt in range(8):
            bank = tt % 4
            def emitv(E, tk, tt=tt, bank=bank):
                for dc in range(NCK):
                    E.matmul(ps[bank][:, 0:256],
                             xbuf[:, dc * T + tt * 128: dc * T + (tt + 1) * 128],
                             pool[:, WS1 + dc * 256: WS1 + (dc + 1) * 256],
                             start=(dc == 0), stop=False)
                tk(E.matmul(ps[bank][:, 0:256],
                            statr[0:1, S_MU * 1024 + tt * 128:S_MU * 1024 + (tt + 1) * 128],
                            wcs(0, 256), start=False, stop=True))
            top(emitv, reads=[RX[dc] for dc in range(NCK)] + zones(RP, WS1, WS1 + 4096)
                + RWC + [RST[S_MU]], writes=[RPS[bank]])
            vop(lambda E, tk, tt=tt, bank=bank: tk(E.tensor_scalar(
                pool[:, VST + tt * 256: VST + (tt + 1) * 256], ps[bank][:, 0:256],
                rtm[:, tt:tt + 1], None, op0=ALU.mult)),
                reads=[RPS[bank], Rrtm],
                writes=zones(RP, VST + tt * 256, VST + (tt + 1) * 256))
        for sg in range(2):
            def store_shv0(E, tk, sg=sg):
                rv = get_qb(E, sg)
                src = pool[:, VST + sg * 1024:VST + (sg + 1) * 1024].rearrange(
                    "p (t j) -> p t j", t=4)
                dst = shv0[DynSlice(rv, 512), :].rearrange("(t p) j -> p t j", p=128)
                tk(E.dma_start(out=dst, in_=src))
            gop(store_shv0, reads=zones(RP, VST + sg * 1024, VST + (sg + 1) * 1024),
                writes=[Rshv0], sem="GX")
        if KDUMP == 'sem':
            bdump1(1030, 'LD')
            bdump1(1031, 'GD')
            bdump1(1032, 'GX')
            bdump1(1033, 'PE')
            bdump1(1034, 'DV')
            bdump1(1035, 'AC')
            bdump1(1036, 'GP')
        if KDUMP == 'sent':
            bdump(1027, statr[0:1, 0:1024], 1)
        if KDUMP == "kv":
            B.op("gpsimd", lambda E, tk: tk(E.dma_start(
                out=out[256:512, :].rearrange("(c p) t -> p c t", p=128),
                in_=pool[:, VST:VST + 2048].rearrange("p (c t) -> p c t", c=2))),
                reads=zones(RP, VST, VST + 2048), writes=(), sem="GD")

        for h in range(KH or H0):
            wsb = WS0 if h % 2 == 0 else WS1
            wz = zones(RP, wsb, wsb + 4096)
            ld(pool[:, wsb:wsb + 4096], l0_qT[h], wz)
            if KDUMP == "w" and h == 0:
                bdump(0, pool[:, wsb:wsb + 1024], 128)
                bdump(256, cost[:, 0:1024], 128)
                bdump(384, identt[:], 128)
                bdump(512, crow[:], 1)
                bdump(513, qnw[0:1, :], 1)
                bdump(640, xbuf[:, 0:1024], 128)
            wcolsum(wsb, 256, 256, True)
            for ch in range(2):
                for th in range(2):
                    bank = ch * 2 + th
                    def emitq(E, tk, ch=ch, th=th, bank=bank, wsb=wsb):
                        for dc in range(NCK):
                            E.matmul(ps[bank][:, 0:512],
                                     pool[:, wsb + dc * 256 + ch * 128: wsb + dc * 256 + ch * 128 + 128],
                                     xbuf[:, dc * T + th * 512: dc * T + (th + 1) * 512],
                                     start=(dc == 0), stop=False)
                        tk(E.matmul(ps[bank][:, 0:512],
                                    wcs(ch * 128, ch * 128 + 128),
                                    srow(S_MU)[:, th * 512:(th + 1) * 512],
                                    start=False, stop=True))
                    top(emitq, reads=[RX[dc] for dc in range(NCK)] + wz + RWC + [RST[S_MU]],
                        writes=[RPS[bank]])
                    vop(lambda E, tk, b=bank, ch=ch, th=th: tk(E.tensor_tensor(
                        pool[:, QH + ch * 1024 + th * 512: QH + ch * 1024 + (th + 1) * 512],
                        ps[b][:, 0:512], Rbuf[:, th * 512:(th + 1) * 512], op=ALU.mult)),
                        reads=[RPS[bank], RRB[0]],
                        writes=zones(RP, QH + ch * 1024 + th * 512, QH + ch * 1024 + (th + 1) * 512))
            head_rms(QH, b_e6, 1.0 / 256)
            for ch in range(2):
                vop(lambda E, tk, ch=ch: tk(E.tensor_scalar(
                    pool[:, QH + ch * 1024:QH + (ch + 1) * 1024],
                    pool[:, QH + ch * 1024:QH + (ch + 1) * 1024],
                    qnw[:, ch:ch + 1], None, op0=ALU.mult)),
                    reads=[Rsmall], writes=zones(RP, QH + ch * 1024, QH + (ch + 1) * 1024))
            rope6(QH, KST, TMPA, TMPB, 1024)
            bcast_hd_mul(QH, 2)
            B.op("gpsimd", lambda E, tk, h=h: tk(E.dma_start(
                out=qs[h * 256:(h + 1) * 256, :].rearrange("(c p) t -> p c t", p=128),
                in_=pool[:, QH:QH + 2048].rearrange("p (c t) -> p c t", c=2))),
                reads=zones(RP, QH, QH + 2048), writes=[Rqs], sem="GD")
            if KDUMP == "q" and h == 0:
                for ch in range(2):
                    B.op("gpsimd", lambda E, tk, ch=ch: tk(E.dma_start(
                        out=out[ch * 128:(ch + 1) * 128, :],
                        in_=pool[:, QH + ch * 1024:QH + (ch + 1) * 1024])),
                        reads=zones(RP, QH, QH + 2048), writes=(), sem="GD")

        def barrier(kv_regs):
            gop(lambda E, tk: tk(E.dma_start(out=bar_in[:], in_=statr[0:1, 0:8])),
                reads=[RST[S_MU]], writes=[Rbar], sem="GD")
            return B.op("gpsimd", lambda E, tk: tk(E.collective_compute(
                "AllGather", ALU.bypass, replica_groups=[list(range(8))],
                ins=[bar_in.opt()], outs=[bar_out.opt()])),
                reads=[Rbar], writes=[Rbar] + kv_regs, sem="CC")

        barrier([Rshk0, Rshv0])

        CTX = 0
        KB = [8192, 9216]
        VB = [10240, 11264]
        QHH = 12288
        PT = [13312, 13824]
        SG = 14336
        MT = 14848

        def attention(layer):
            if os.environ.get("KTRACE"):
                print("ATTENTION CALLED", layer)
            H = H0 if layer == 0 else H1
            HDC = 2 if layer == 0 else 1
            HD = HD0 if layer == 0 else HD1
            mb = b_m0 if layer == 0 else b_m1
            ksrc, vsrc = (shk0, shv0) if layer == 0 else (shk1, shv1)
            krg, vrg = (Rshk0, Rshv0) if layer == 0 else (Rshk1, Rshv1)
            for g in range(2):
                nblk = (2, 4)[g]
                ntk = (8, 16)[g]

                def head_tiles(h, g=g, nblk=nblk, ntk=ntk):
                    hp = h % 2
                    QHHp = QHH if hp == 0 else 15360
                    if layer == 0:
                        qsl = qs[h * 256:(h + 1) * 256, g * 512:(g + 1) * 512].rearrange(
                            "(c p) t -> p c t", p=128)
                        qdst = pool[:, QHHp:QHHp + 1024].rearrange("p (c t) -> p c t", c=2)
                    else:
                        qsl = qs[h * 128:(h + 1) * 128, g * 512:(g + 1) * 512]
                        qdst = pool[:, QHHp:QHHp + 512]
                    ld(qdst, qsl, zones(RP, QHHp, QHHp + HDC * 512))
                    qz = zones(RP, QHHp, QHHp + HDC * 512)
                    ctx_banks = ([4, 5] if hp == 0 else [6, 7]) if HDC == 2 else [4 + hp]
                    first_ctx = True
                    lparts = {i: [] for i in range(4)}
                    for kb in range(nblk):
                        kbr = kb % 2
                        if layer == 0:
                            kap = ksrc[:, kb * 512:(kb + 1) * 512].rearrange(
                                "(c p) t -> p c t", p=128)
                            kdst = pool[:, KB[kbr]:KB[kbr] + 1024].rearrange(
                                "p (c t) -> p c t", c=2)
                            vap = vsrc[kb * 512:(kb + 1) * 512, :].rearrange(
                                "(t p) j -> p t j", p=128)
                            vdst = pool[:, VB[kbr]:VB[kbr] + 1024].rearrange(
                                "p (t j) -> p t j", t=4)
                        else:
                            kap = ksrc[h * 128:(h + 1) * 128, kb * 512:(kb + 1) * 512]
                            kdst = pool[:, KB[kbr]:KB[kbr] + 512]
                            vap = vsrc[kb * 512:(kb + 1) * 512, h * 128:(h + 1) * 128].rearrange(
                                "(t p) j -> p t j", p=128)
                            vdst = pool[:, VB[kbr]:VB[kbr] + 512].rearrange(
                                "p (t j) -> p t j", t=4)
                        kz = zones(RP, KB[kbr], KB[kbr] + 1024)
                        vz = zones(RP, VB[kbr], VB[kbr] + 1024)
                        B.op("sync", lambda E, tk, d=kdst, s=kap: tk(E.dma_start(out=d, in_=s)),
                             reads=[krg], writes=kz, sem="LD")
                        B.op("sync", lambda E, tk, d=vdst, s=vap: tk(E.dma_start(out=d, in_=s)),
                             reads=[vrg], writes=vz, sem="LD")
                        for tkt in range(4):
                            kt = kb * 4 + tkt
                            if kt >= ntk:
                                continue
                            tbank = 2 + (kt % 2)
                            def emitT(E, tk, tkt=tkt, kbr=kbr, tbank=tbank, QHHp=QHHp):
                                for ch in range(HDC):
                                    ins = E.matmul(
                                        ps[tbank][:, 0:512],
                                        pool[:, KB[kbr] + ch * 512 + tkt * 128:
                                             KB[kbr] + ch * 512 + tkt * 128 + 128],
                                        pool[:, QHHp + ch * 512:QHHp + (ch + 1) * 512],
                                        start=(ch == 0), stop=(ch == HDC - 1))
                                tk(ins)
                            top(emitT, reads=qz + kz, writes=[RPS[tbank]])
                            ptr = kt % 2
                            aop(lambda E, tk, tbank=tbank, ptr=ptr: tk(E.activation(
                                pool[:, PT[ptr]:PT[ptr] + 512], ps[tbank][:, 0:512], AF.Exp,
                                bias=mb[:])),
                                reads=[RPS[tbank], Rsmall],
                                writes=zones(RP, PT[ptr], PT[ptr] + 512))
                            minqb = (0, 1024)[g]
                            need_mask = not (minqb - 128 * kt >= 127)
                            if need_mask:
                                sl = slot_of(("T", g, kt))
                                vop(lambda E, tk, sl=sl, ptr=ptr: tk(E.scalar_tensor_tensor(
                                    pool[:, PT[ptr]:PT[ptr] + 512], djp[:],
                                    cbc[:, sl:sl + 1], pool[:, PT[ptr]:PT[ptr] + 512],
                                    op0=ALU.is_ge, op1=ALU.mult)),
                                    reads=[Rcbc, Rsmall],
                                    writes=zones(RP, PT[ptr], PT[ptr] + 512))
                            pt_c, pt_z = PT[ptr], zones(RP, PT[ptr], PT[ptr] + 512)
                            for ci, cbank in enumerate(ctx_banks):
                                def emitC(E, tk, ci=ci, cbank=cbank, tkt=tkt, kbr=kbr,
                                          pt_c=pt_c, first=first_ctx, last=(kt == ntk - 1)):
                                    ins = E.matmul(
                                        ps[cbank][:, 0:512],
                                        pool[:, VB[kbr] + tkt * HD + ci * 128:
                                             VB[kbr] + tkt * HD + ci * 128 + 128],
                                        pool[:, pt_c:pt_c + 512],
                                        start=first, stop=last)
                                    tk(ins)
                                top(emitC, reads=vz + pt_z, writes=[RPS[cbank]])
                            acc_c = SG if hp == 0 else MT
                            acc_z = zones(RP, acc_c, acc_c + 512)
                            if first_ctx:
                                vop(lambda E, tk, pt_c=pt_c, acc_c=acc_c: tk(E.tensor_copy(
                                    pool[:, acc_c:acc_c + 512], pool[:, pt_c:pt_c + 512])),
                                    reads=pt_z, writes=acc_z)
                            else:
                                vop(lambda E, tk, pt_c=pt_c, acc_c=acc_c: tk(E.tensor_tensor(
                                    pool[:, acc_c:acc_c + 512], pool[:, acc_c:acc_c + 512],
                                    pool[:, pt_c:pt_c + 512], op=ALU.add)),
                                    reads=pt_z + acc_z, writes=acc_z)
                            first_ctx = False

                def head_evac(h, g=g):
                    hp = h % 2
                    hz = S_HD if hp == 0 else S_SG
                    HR = RST[S_HD] if hp == 0 else RST[S_SG]
                    RRTp = RRT if hp == 0 else Rcbc2
                    acc_c = SG if hp == 0 else MT
                    acc_z = zones(RP, acc_c, acc_c + 512)
                    ctx_banks = ([4, 5] if hp == 0 else [6, 7]) if HDC == 2 else [4 + hp]
                    top(lambda E, tk, hp=hp, acc_c=acc_c: tk(E.matmul(
                        ps[hp][0:1, 0:512], ones_p[:], pool[:, acc_c:acc_c + 512],
                        start=True, stop=True)),
                        reads=acc_z + [Rsmall], writes=[RPS[hp]])
                    aop(lambda E, tk, hz=hz, hp=hp: tk(E.copy(
                        statr[0:1, hz * 1024:hz * 1024 + 512], ps[hp][0:1, 0:512])),
                        reads=[RPS[hp]], writes=[HR])
                    vop(lambda E, tk, hz=hz: tk(E.reciprocal(
                        statr[0:1, hz * 1024 + 512:hz * 1024 + 1024],
                        statr[0:1, hz * 1024:hz * 1024 + 512])),
                        reads=[HR], writes=[HR])
                    bcastf(lambda hp=hp: Rtmp[:, hp * 512:hp * 512 + 512], [RRTp],
                           lambda hz=hz: statr[0:1, hz * 1024 + 512:hz * 1024 + 1024],
                           [HR], 512, hp)
                    for ci, cbank in enumerate(ctx_banks):
                        cg = h * HDC + ci
                        vop(lambda E, tk, cbank=cbank, cg=cg, hp=hp: tk(E.tensor_tensor(
                            pool[:, CTX + cg * 512: CTX + (cg + 1) * 512],
                            ps[cbank][:, 0:512], Rtmp[:, hp * 512:hp * 512 + 512],
                            op=ALU.mult)),
                            reads=[RPS[cbank], RRTp],
                            writes=zones(RP, CTX + cg * 512, CTX + (cg + 1) * 512))
                    if KDUMP == "att" and h == 0 and layer == 0:
                        bdump(1280 + g * 128, pool[:, CTX:CTX + 512], 128)
                        bdump(1536 + g * 128, lsum[:, 0:48], 128)
                        bdump(1792 + g, statr[0:1, S_HD * 1024:S_HD * 1024 + 1024], 1)
                        if g == 0:
                            bdump(1800, djp[:], 128)
                            bdump(1930, cbc[0:2, 0:64], 2)

                NH = KH or H
                for h in range(NH):
                    head_tiles(h)
                    if h >= 1:
                        head_evac(h - 1)
                head_evac(NH - 1)
                if KDUMP == "l1" and layer == 1 and g == 0:
                    for ii in range(8):
                        bdump(768 + ii * 128, pool[:, CTX + ii * 1024:CTX + (ii + 1) * 1024], 128)
                if KDUMP == "ctx" and layer == 0 and g == 0:
                    for ii in range(8):
                        bdump(ii * 128, pool[:, CTX + ii * 1024:CTX + (ii + 1) * 1024], 128)
                OW = [8192, 12288]
                wsrc = l0_oT if layer == 0 else l1_wo
                for sl8 in range(8):
                    owb = OW[sl8 % 2]
                    owz = zones(RP, owb, owb + 4096)
                    ld(pool[:, owb:owb + 4096], wsrc[sl8], owz)
                    for fc2 in range(2):
                        fglob = sl8 * 2 + fc2
                        bank = fc2 % 2
                        def emitO(E, tk, owb=owb, fc2=fc2, bank=bank):
                            for dc in range(NCK):
                                ins = E.matmul(
                                    ps[bank][:, 0:512],
                                    pool[:, owb + dc * 256 + fc2 * 128:
                                         owb + dc * 256 + fc2 * 128 + 128],
                                    pool[:, CTX + dc * 512:CTX + (dc + 1) * 512],
                                    start=(dc == 0), stop=(dc == NCK - 1))
                            tk(ins)
                        top(emitO, reads=owz + zones(RP, CTX, CTX + 8192), writes=[RPS[bank]])
                        xsl = lambda fglob=fglob: xbuf[:, fglob * T + g * 512: fglob * T + g * 512 + 512]
                        if layer == 0:
                            vop(lambda E, tk, bank=bank, fglob=fglob, g=g: tk(E.tensor_tensor(
                                xbuf[:, fglob * T + g * 512: fglob * T + g * 512 + 512],
                                ps[bank][:, 0:512],
                                xbuf[:, fglob * T + g * 512: fglob * T + g * 512 + 512],
                                op=ALU.add)),
                                reads=[RPS[bank], RX[fglob]], writes=[RX[fglob]])
                        else:
                            vop(lambda E, tk, bank=bank, fglob=fglob, g=g: tk(E.scalar_tensor_tensor(
                                xbuf[:, fglob * T + g * 512: fglob * T + g * 512 + 512],
                                ps[bank][:, 0:512], bot[:, fglob:fglob + 1],
                                xbuf[:, fglob * T + g * 512: fglob * T + g * 512 + 512],
                                op0=ALU.add, op1=ALU.add)),
                                reads=[RPS[bank], RX[fglob], Rsmall], writes=[RX[fglob]])

        attention(0)

    if KSTAGE >= 3:
        layernorm_stats()
        vop(lambda E, tk: tk(E.tensor_scalar(srow(S_HD), srow(S_RI), scv[0:1, 0:1], None,
                                             op0=ALU.mult)),
            reads=[RST[S_RI], Rscv], writes=[RST[S_HD]])
        bcastf(lambda: Rbuf[:, 2048:3072], [RRB[2]], lambda: srow(S_HD), [RST[S_HD]], 1024, 6)
        vop(lambda E, tk: tk(E.reciprocal(statr[0:1, S_HD * 1024:S_HD * 1024 + 1],
                                          scv[0:1, 0:1])),
            reads=[Rscv], writes=[RST[S_HD]])
        vop(lambda E, tk: tk(E.reciprocal(statr[0:1, S_HD * 1024 + 1:S_HD * 1024 + 2],
                                          scv[0:1, 1:2])),
            reads=[Rscv], writes=[RST[S_HD]])
        vop(lambda E, tk: tk(E.tensor_tensor(statr[0:1, S_HD * 1024:S_HD * 1024 + 1],
                                             statr[0:1, S_HD * 1024:S_HD * 1024 + 1],
                                             statr[0:1, S_HD * 1024 + 1:S_HD * 1024 + 2],
                                             op=ALU.mult)),
            reads=[RST[S_HD]], writes=[RST[S_HD]])
        bcastf(lambda: f4b, [Rf4], lambda: statr[0:1, S_HD * 1024:S_HD * 1024 + 1],
               [RST[S_HD]], 1, 6)
        bcastf(lambda: sc1b, [Rf4], lambda: scv[0:1, 1:2], [Rscv], 1, 6)
        TQ0, TQ1 = 8192, 9216
        for dc in range(NCK):
            tq = TQ0 if dc % 2 == 0 else TQ1
            vop(lambda E, tk, dc=dc, tq=tq: tk(E.tensor_tensor(
                pool[:, tq:tq + 1024], xbuf[:, dc * T:(dc + 1) * T], Rbuf[:, 1024:2048],
                op=ALU.add)), reads=[RX[dc], RRB[1]], writes=zones(RP, tq, tq + 1024))
            vop(lambda E, tk, dc=dc, tq=tq: tk(E.tensor_tensor(
                f8a[:, dc * T:(dc + 1) * T], pool[:, tq:tq + 1024], Rbuf[:, 2048:3072],
                op=ALU.mult)), reads=zones(RP, tq, tq + 1024) + [RRB[2]],
                writes=[RF8[dc // 4]])
        for sl8 in range(8):
            wb = WS0 if sl8 % 2 == 0 else WS1
            wz = zones(RP, wb, wb + 4096)
            f8r = 0
            ld(pool[:, wb:wb + 4096], l0_w2[sl8], wz)
            vop(lambda E, tk, wb=wb: tk(E.tensor_scalar(
                f8b[:, 0:4096], pool[:, wb:wb + 4096],
                sc1b[:], None, op0=ALU.mult)),
                reads=wz + [Rf4], writes=[RF8B[0]])
            for nc2 in range(2):
                fglob = sl8 * 2 + nc2
                for th in range(2):
                    bank = nc2 * 2 + th
                    def emitF(E, tk, f8r=f8r, nc2=nc2, th=th, bank=bank):
                        for kc in range(NCK):
                            ins = E.matmul(
                                ps[bank][:, 0:512],
                                f8b[:, f8r * 4096 + kc * 256 + nc2 * 128:
                                    f8r * 4096 + kc * 256 + nc2 * 128 + 128],
                                f8a[:, kc * T + th * 512: kc * T + th * 512 + 512],
                                start=(kc == 0), stop=(kc == NCK - 1))
                        tk(ins)
                    top(emitF, reads=[RF8B[f8r]] + RF8, writes=[RPS[bank]])
                    vop(lambda E, tk, bank=bank, fglob=fglob, th=th: tk(E.scalar_tensor_tensor(
                        xbuf[:, fglob * T + th * 512: fglob * T + th * 512 + 512],
                        ps[bank][:, 0:512], f4b[:],
                        xbuf[:, fglob * T + th * 512: fglob * T + th * 512 + 512],
                        op0=ALU.mult, op1=ALU.add)),
                        reads=[RPS[bank], RX[fglob], Rf4], writes=[RX[fglob]])

    if KSTAGE >= 4:
        layernorm_stats()
        vop(lambda E, tk: tk(E.tensor_scalar(srow(S_HD), srow(S_RI), 1.0 / math.sqrt(128.0),
                                             None, op0=ALU.mult)),
            reads=[RST[S_RI]], writes=[RST[S_HD]])
        bcastf(lambda: Rbuf[:, 2048:3072], [RRB[2]], lambda: srow(S_HD), [RST[S_HD]], 1024, 6)

        for h in range(KH or H1):
            for which in range(2):
                wsrc = l1_qT if which == 0 else l1_kT
                brsrc = l1_bqr if which == 0 else l1_bkr
                wsb = WS0 if (h * 2 + which) % 2 == 0 else WS1
                wz = zones(RP, wsb, wsb + 2048)
                ld(pool[:, wsb:wsb + 2048], wsrc[h], wz)
                B.op("sync", lambda E, tk, h=h, brsrc=brsrc: tk(E.dma_start(
                    out=brstage[0:1, 0:128], in_=brsrc[0:1, h * 128:(h + 1) * 128])),
                    reads=(), writes=[Rbrs], sem="LD")
                wcolsum(wsb, 128, 128, True)
                for th in range(2):
                    bank = th
                    def emitqk(E, tk, th=th, bank=bank, wsb=wsb):
                        for dc in range(NCK):
                            E.matmul(ps[bank][:, 0:512],
                                     pool[:, wsb + dc * 128: wsb + dc * 128 + 128],
                                     xbuf[:, dc * T + th * 512: dc * T + (th + 1) * 512],
                                     start=(dc == 0), stop=False)
                        E.matmul(ps[bank][:, 0:512],
                                 wcs(0, 128),
                                 srow(S_MU)[:, th * 512:(th + 1) * 512],
                                 start=False, stop=False)
                        tk(E.matmul(ps[bank][:, 0:512],
                                    brstage[0:1, 0:128],
                                    srow(S_SG)[:, th * 512:(th + 1) * 512],
                                    start=False, stop=True))
                    top(emitqk, reads=[RX[dc] for dc in range(NCK)] + wz + RWC
                        + [RST[S_MU], RST[S_SG], Rbrs], writes=[RPS[bank]])
                    rb = 2 if which == 0 else 0
                    koff = VST if which == 0 else KST + (h % 2) * 1024
                    vop(lambda E, tk, bank=bank, th=th, rb=rb, koff=koff: tk(E.tensor_tensor(
                        pool[:, koff + th * 512: koff + (th + 1) * 512],
                        ps[bank][:, 0:512], Rbuf[:, rb * 1024 + th * 512: rb * 1024 + (th + 1) * 512],
                        op=ALU.mult)),
                        reads=[RPS[bank], RRB[rb]],
                        writes=zones(RP, koff + th * 512, koff + (th + 1) * 512))
                if which == 0:
                    if KDUMP == "l1" and h == 0:
                        bdump(0, pool[:, VST:VST + 1024], 128)
                    B.op("gpsimd", lambda E, tk, h=h: tk(E.dma_start(
                        out=qs[h * 128:(h + 1) * 128, :], in_=pool[:, VST:VST + 1024])),
                        reads=zones(RP, VST, VST + 1024), writes=[Rqs], sem="GD")
                elif h % 2 == 1:
                    if KDUMP == "l1" and h == 1:
                        bdump(128, pool[:, KST:KST + 1024], 128)
                    def store_k1(E, tk, h=h):
                        tk(E.dma_start(
                            out=k1scr[(h - 1) * 128:(h + 1) * 128, :].rearrange(
                                "(c p) t -> p c t", p=128),
                            in_=pool[:, KST:KST + 2048].rearrange("p (c t) -> p c t", c=2)))
                    gop(store_k1, reads=zones(RP, KST, KST + 2048), writes=[Rk1s], sem="GX")

        for sg in range(2):
            def k1_bulk(E, tk, sg=sg):
                rv = get_qb(E, sg)
                tk(E.dma_start(out=shk1[:, DynSlice(rv, 512)],
                               in_=k1scr[:, sg * 512:(sg + 1) * 512]))
            gop(k1_bulk, reads=[Rk1s], writes=[Rshk1], sem="GX")

        for vs in range(4):
            wsb = WS0
            wz = zones(RP, wsb, wsb + 8192)
            ld(pool[:, wsb:wsb + 8192], l1_vT[vs], wz)
            B.op("sync", lambda E, tk, vs=vs: tk(E.dma_start(
                out=brstage[0:1, 0:512], in_=l1_bvr[0:1, vs * 512:(vs + 1) * 512])),
                reads=(), writes=[Rbrs], sem="LD")
            wcolsum(wsb, 512, 512, True)
            for tt in range(8):
                bank = tt % 4
                def emitv1(E, tk, tt=tt, bank=bank, wsb=wsb, vs=vs):
                    for dc in range(NCK):
                        E.matmul(ps[bank][:, 0:512],
                                 xbuf[:, dc * T + tt * 128: dc * T + (tt + 1) * 128],
                                 pool[:, wsb + dc * 512: wsb + (dc + 1) * 512],
                                 start=(dc == 0), stop=False)
                    E.matmul(ps[bank][:, 0:512],
                             statr[0:1, S_MU * 1024 + tt * 128:S_MU * 1024 + (tt + 1) * 128],
                             wcs(0, 512), start=False, stop=False)
                    tk(E.matmul(ps[bank][:, 0:512],
                                statr[0:1, S_SG * 1024 + tt * 128:S_SG * 1024 + (tt + 1) * 128],
                                brstage[0:1, 0:512],
                                start=False, stop=True))
                top(emitv1, reads=[RX[dc] for dc in range(NCK)] + wz + RWC
                    + [RST[S_MU], RST[S_SG], Rbrs], writes=[RPS[bank]])
                vop(lambda E, tk, tt=tt, bank=bank: tk(E.tensor_scalar(
                    pool[:, KST + tt * 512: KST + tt * 512 + 512],
                    ps[bank][:, 0:512], rtm[:, tt:tt + 1], None, op0=ALU.mult)),
                    reads=[RPS[bank], Rrtm],
                    writes=zones(RP, KST + tt * 512, KST + tt * 512 + 512))
            if KDUMP == "l1" and vs == 0:
                for ii in range(4):
                    bdump(256 + ii * 128, pool[:, KST + ii * 1024:KST + (ii + 1) * 1024], 128)
            for sg in range(2):
                def store_v1(E, tk, vs=vs, sg=sg):
                    rv = get_qb(E, sg)
                    tk(E.dma_start(
                        out=shv1[DynSlice(rv, 512), vs * 512:(vs + 1) * 512].rearrange(
                            "(t p) j -> p t j", p=128),
                        in_=pool[:, KST + sg * 2048:KST + (sg + 1) * 2048].rearrange(
                            "p (t j) -> p t j", t=4)))
                gop(store_v1, reads=zones(RP, KST + sg * 2048, KST + (sg + 1) * 2048),
                    writes=[Rshv1], sem="GX")

        barrier([Rshk1, Rshv1])
        attention(1)

    if KSTAGE >= 5:
        layernorm_stats()
        TQ0, TQ1 = 14336, 15360
        for dc in range(NCK):
            tq = TQ0 if dc % 2 == 0 else TQ1
            vop(lambda E, tk, dc=dc, tq=tq: tk(E.tensor_tensor(
                pool[:, tq:tq + 1024], xbuf[:, dc * T:(dc + 1) * T], Rbuf[:, 1024:2048],
                op=ALU.add)), reads=[RX[dc], RRB[1]], writes=zones(RP, tq, tq + 1024))
            for tt in range(8):
                top(lambda E, tk, tq=tq, tt=tt: tk(E.transpose(
                    ps[6][:, (tt % 4) * 128:(tt % 4) * 128 + 128],
                    pool[:, tq + tt * 128:tq + (tt + 1) * 128], identt[:])),
                    reads=zones(RP, tq, tq + 1024) + [Rsmall], writes=[RPS[6]])
                vop(lambda E, tk, tt=tt: tk(E.tensor_reduce(
                    wtm[:, tt:tt + 1], ps[6][:, (tt % 4) * 128:(tt % 4) * 128 + 128],
                    axis=AX.X, op=ALU.max, apply_absolute_value=True)),
                    reads=[RPS[6]], writes=[Rqtm])
            for tt in range(8):
                top(lambda E, tk, tt=tt: tk(E.transpose(
                    ps[7][0:1, (tt % 4) * 128:(tt % 4) * 128 + 128],
                    wtm[:, tt:tt + 1], identt[:])),
                    reads=[Rqtm, Rsmall], writes=[RPS[7]])
                aop(lambda E, tk, tt=tt: tk(E.copy(
                    statr[0:1, S_HD * 1024 + tt * 128:S_HD * 1024 + (tt + 1) * 128],
                    ps[7][0:1, (tt % 4) * 128:(tt % 4) * 128 + 128])),
                    reads=[RPS[7]], writes=[RST[S_HD]])
            vop(lambda E, tk: tk(E.tensor_tensor(
                statr[0:1, S_HD * 1024:S_HD * 1024 + 1024],
                statr[0:1, S_HD * 1024:S_HD * 1024 + 1024], srow(S_RI), op=ALU.mult)),
                reads=[RST[S_RI], RST[S_HD]], writes=[RST[S_HD]])
            vop(lambda E, tk: tk(E.tensor_scalar(
                statr[0:1, S_HD * 1024:S_HD * 1024 + 1024],
                statr[0:1, S_HD * 1024:S_HD * 1024 + 1024], 1e-4, None, op0=ALU.max)),
                reads=[RST[S_HD]], writes=[RST[S_HD]])
            vop(lambda E, tk: tk(E.tensor_scalar(
                dqrows[0:1, 0:1024],
                statr[0:1, S_HD * 1024:S_HD * 1024 + 1024], 1.0 / 224.0, None, op0=ALU.mult)),
                reads=[RST[S_HD]], writes=[Rdq])
            B.op("gpsimd", lambda E, tk, dc=dc: tk(E.dma_start(
                out=dqd[dc:dc + 1, :], in_=dqrows[0:1, 0:1024])),
                reads=[Rdq], writes=[Rdq], sem="GD")
            vop(lambda E, tk: tk(E.reciprocal(
                statr[0:1, S_SG * 1024:S_SG * 1024 + 1024],
                statr[0:1, S_HD * 1024:S_HD * 1024 + 1024])),
                reads=[RST[S_HD]], writes=[RST[S_SG]])
            vop(lambda E, tk: tk(E.tensor_tensor(
                statr[0:1, S_SG * 1024:S_SG * 1024 + 1024],
                statr[0:1, S_SG * 1024:S_SG * 1024 + 1024], srow(S_RI),
                op=ALU.mult)), reads=[RST[S_RI], RST[S_SG]], writes=[RST[S_SG]])
            vop(lambda E, tk: tk(E.tensor_scalar(
                statr[0:1, S_SG * 1024:S_SG * 1024 + 1024],
                statr[0:1, S_SG * 1024:S_SG * 1024 + 1024], 224.0, None,
                op0=ALU.mult)), reads=[RST[S_SG]], writes=[RST[S_SG]])
            bcastf(lambda: Rtmp[:, 0:1024], [RRT],
                   lambda: statr[0:1, S_SG * 1024:S_SG * 1024 + 1024], [RST[S_SG]], 1024, 6)
            vop(lambda E, tk, dc=dc, tq=tq: tk(E.tensor_tensor(
                f8a[:, dc * T:(dc + 1) * T], pool[:, tq:tq + 1024], Rtmp[:, 0:1024],
                op=ALU.mult)), reads=zones(RP, tq, tq + 1024) + [RRT],
                writes=[RF8[dc // 4]])
        if KDUMP == "f1":
            bdump(1024, statr[0:1, S_HD * 1024:S_HD * 1024 + 1024], 1)
            bdump(1025, statr[0:1, S_SG * 1024:S_SG * 1024 + 1024], 1)
            bdump(1026, dqrows[0:1, 0:1024], 1)
            bdump(1027, statr[0:1, S_RI * 1024:S_RI * 1024 + 1024], 1)
        XD = 0
        WSL = [8192, 10240]
        WD = 12288
        for g in range(2):
            for dc in range(NCK):
                p2 = dc % 2
                RDQ = Rdq if p2 == 0 else Rdq2
                RRTp = RRT if p2 == 0 else Rcbc2
                B.op("sync", lambda E, tk, dc=dc, g=g, p2=p2: tk(E.dma_start(
                    out=dqrows[0:1, p2 * 512:(p2 + 1) * 512],
                    in_=dqd[dc:dc + 1, g * 512:g * 512 + 512])),
                    reads=[RDQ], writes=[RDQ], sem="LD")
                bcastf(lambda p2=p2: Rtmp[:, p2 * 512:(p2 + 1) * 512], [RRTp],
                       lambda p2=p2: dqrows[0:1, p2 * 512:(p2 + 1) * 512],
                       [RDQ], 512, 6 + p2)
                vop(lambda E, tk, dc=dc, g=g, p2=p2, RRTp=RRTp: tk(E.tensor_tensor(
                    pool[:, XD + dc * 512: XD + (dc + 1) * 512],
                    f8a[:, dc * T + g * 512: dc * T + g * 512 + 512],
                    Rtmp[:, p2 * 512:(p2 + 1) * 512],
                    op=ALU.mult)), reads=[RF8[dc // 4], RRTp],
                    writes=zones(RP, XD + dc * 512, XD + (dc + 1) * 512))
            def ldW(sl16):
                p = sl16 % 2
                wb = WSL[p]
                ld(pool[:, wb:wb + 2048], l1_wT[sl16], zones(RP, wb, wb + 2048))

            def stageA(sl16):
                p = sl16 % 2
                wb = WSL[p]
                wz = zones(RP, wb, wb + 2048)
                RW = Rwtm if p == 0 else Rwtm2
                RWC_ = RST[S_WC] if p == 0 else Rwc1
                RCB = Rcbc if p == 0 else Rcbc2
                wcb = S_WC * 1024 + p * 96
                cb0 = 16 + p * 48
                pb = 6 + p
                for dc in range(NCK):
                    vop(lambda E, tk, wb=wb, dc=dc, p=p: tk(E.tensor_reduce(
                        wtm[:, p * 16 + dc:p * 16 + dc + 1],
                        pool[:, wb + dc * 128:wb + (dc + 1) * 128],
                        axis=AX.X, op=ALU.max, apply_absolute_value=True)),
                        reads=wz, writes=[RW])
                top(lambda E, tk, p=p, pb=pb: tk(E.transpose(
                    ps[pb][0:16, 0:128], wtm[:, p * 16:(p + 1) * 16], identt[:])),
                    reads=[RW, Rsmall], writes=[RPS[pb]])
                vop(lambda E, tk, p=p, pb=pb: tk(E.tensor_reduce(
                    wtm2[:, p:p + 1], ps[pb][0:16, 0:128], axis=AX.X, op=ALU.max)),
                    reads=[RPS[pb]], writes=[RW])
                top(lambda E, tk, p=p, pb=pb: tk(E.transpose(
                    ps[pb][0:1, 128:144], wtm2[:, p:p + 1], identt[0:16, 0:16])),
                    reads=[RW, Rsmall], writes=[RPS[pb]])
                aop(lambda E, tk, wcb=wcb, pb=pb: tk(E.copy(
                    statr[0:1, wcb:wcb + 16], ps[pb][0:1, 128:144])),
                    reads=[RPS[pb]], writes=[RWC_])
                vop(lambda E, tk, wcb=wcb: tk(E.tensor_scalar(
                    statr[0:1, wcb:wcb + 16],
                    statr[0:1, wcb:wcb + 16], 1e-4, None, op0=ALU.max)),
                    reads=[RWC_], writes=[RWC_])
                vop(lambda E, tk, wcb=wcb: tk(E.reciprocal(
                    statr[0:1, wcb + 16:wcb + 32], statr[0:1, wcb:wcb + 16])),
                    reads=[RWC_], writes=[RWC_])
                vop(lambda E, tk, wcb=wcb: tk(E.tensor_scalar(
                    statr[0:1, wcb + 16:wcb + 32],
                    statr[0:1, wcb + 16:wcb + 32], 224.0, None, op0=ALU.mult)),
                    reads=[RWC_], writes=[RWC_])
                vop(lambda E, tk, wcb=wcb: tk(E.tensor_scalar(
                    statr[0:1, wcb + 32:wcb + 48],
                    statr[0:1, wcb:wcb + 16], 1.0 / 224.0, None, op0=ALU.mult)),
                    reads=[RWC_], writes=[RWC_])
                bcastf(lambda cb0=cb0: cbc[:, cb0:cb0 + 16], [RCB],
                       lambda wcb=wcb: statr[0:1, wcb + 16:wcb + 32], [RWC_], 16, pb)
                bcastf(lambda cb0=cb0: cbc[:, cb0 + 16:cb0 + 32], [RCB],
                       lambda wcb=wcb: statr[0:1, wcb + 32:wcb + 48], [RWC_], 16, pb)

            def stageBpre(sl16):
                p = sl16 % 2
                wb = WSL[p]
                wz = zones(RP, wb, wb + 2048)
                WDp = WD if p == 0 else 14336
                RCB = Rcbc if p == 0 else Rcbc2
                cb0 = 16 + p * 48
                for dc in range(NCK):
                    f8r = p * 2 + dc % 2
                    vop(lambda E, tk, wb=wb, dc=dc, f8r=f8r, cb0=cb0: tk(E.tensor_scalar(
                        f8b[:, f8r * 128:(f8r + 1) * 128],
                        pool[:, wb + dc * 128: wb + (dc + 1) * 128],
                        cbc[:, cb0 + dc:cb0 + dc + 1], None, op0=ALU.mult)),
                        reads=wz + [RCB], writes=[RF8B[p]])
                    vop(lambda E, tk, dc=dc, f8r=f8r, cb0=cb0, WDp=WDp: tk(E.tensor_scalar(
                        pool[:, WDp + dc * 128: WDp + (dc + 1) * 128],
                        f8b[:, f8r * 128:(f8r + 1) * 128],
                        cbc[:, cb0 + 16 + dc:cb0 + 17 + dc], None, op0=ALU.mult)),
                        reads=[RF8B[p], RCB],
                        writes=zones(RP, WDp + dc * 128, WDp + (dc + 1) * 128))

            def stageBmm(sl16):
                p = sl16 % 2
                WDp = WD if p == 0 else 14336
                wdz = zones(RP, WDp, WDp + 2048)
                bank = p
                def emitW(E, tk, bank=bank, WDp=WDp):
                    for dc in range(NCK):
                        ins = E.matmul(ps[bank][:, 0:512],
                                       pool[:, WDp + dc * 128: WDp + dc * 128 + 128],
                                       pool[:, XD + dc * 512: XD + dc * 512 + 512],
                                       start=(dc == 0), stop=(dc == NCK - 1))
                    tk(ins)
                top(emitW, reads=wdz + zones(RP, XD, XD + 8192), writes=[RPS[bank]])

            def stageBevac(sl16, g):
                p = sl16 % 2
                bank = p
                ROB = Robf if p == 0 else Robf2
                aop(lambda E, tk, bank=bank, p=p: tk(E.copy(
                    obf[:, p * 512:(p + 1) * 512], ps[bank][:, 0:512])),
                    reads=[RPS[bank]], writes=[ROB])
                vop(lambda E, tk, sl16=sl16, g=g, p=p: tk(E.tensor_tensor(
                    xbuf[:, sl16 * T + g * 512: sl16 * T + g * 512 + 512],
                    xbuf[:, sl16 * T + g * 512: sl16 * T + g * 512 + 512],
                    obf[:, p * 512:(p + 1) * 512],
                    op=ALU.add)), reads=[RX[sl16], ROB], writes=[RX[sl16]])

            ldW(0)
            stageA(0)
            ldW(1)
            stageA(1)
            stageBpre(0)
            for sl16 in range(16):
                stageBmm(sl16)
                if sl16 + 2 < 16:
                    ldW(sl16 + 2)
                    stageA(sl16 + 2)
                if sl16 + 1 < 16:
                    stageBpre(sl16 + 1)
                stageBevac(sl16, g)

    if not KDUMP:
        for dc in range(NCK):
            B.op("gpsimd", lambda E, tk, dc=dc: tk(E.dma_start(
                out=out[dc * 128:(dc + 1) * 128, :], in_=xbuf[:, dc * T:(dc + 1) * T])),
                reads=[RX[dc]], writes=(), sem="GD")
    for q in ("GD", "GX"):
        for i in range(NQ):
            s = B.sems[q + str(i)]
            if s.c > 0:
                B.plan["gpsimd"].append(lambda E, s=s, v=s.c: E.wait_ge(s.h, v))

    with ExitStack() as ses:
        for n, s in B.sems.items():
            s.h = ses.enter_context(nc.semaphore(n))
        with nc.Block() as block:
            @block.sync
            def _(E):
                for f in B.plan["sync"]:
                    f(E)

            @block.tensor
            def _(E):
                for f in B.plan["tensor"]:
                    f(E)

            @block.vector
            def _(E):
                for f in B.plan["vector"]:
                    f(E)

            @block.scalar
            def _(E):
                for f in B.plan["scalar"]:
                    f(E)

            @block.gpsimd
            def _(E):
                for f in B.plan["gpsimd"]:
                    f(E)
    es.close()
    _cache["nc"] = nc
    return nc


def _slabify(W, ncols):
    F = W.shape[1]
    nsl = F // ncols
    return np.ascontiguousarray(
        W.reshape(16, 128, nsl, ncols).transpose(2, 1, 0, 3).reshape(nsl, 128, 16 * ncols),
        dtype=np.float32)


def prepare(inputs):
    f32 = lambda a: np.ascontiguousarray(a, dtype=np.float32)

    qT = _slabify(f32(inputs["L0_attn_q_proj"]).T, 256)
    kT = _slabify(f32(inputs["L0_attn_k_proj"]).T, 256)
    vT = _slabify(f32(inputs["L0_attn_v_proj"]).T, 256)
    oT = _slabify(f32(inputs["L0_attn_o_proj"]).T, 256)
    w2 = _slabify(f32(inputs["L0_ffn_input2"]), 256)
    qn = f32(inputs["L0_attn_q_norm"]).reshape(2, 128).T.copy()
    kn = f32(inputs["L0_attn_k_norm"]).reshape(2, 128).T.copy()
    sc = f32(inputs["L0_ffn_scale"]).reshape(1, 1)
    sc1 = f32(inputs["L0_ffn_scale1"]).reshape(1, 1)
    w1q = _slabify(f32(inputs["L1_attn_Wq"]).transpose(1, 0, 2).reshape(D, D), 128)
    w1k = _slabify(f32(inputs["L1_attn_Wk"]).transpose(1, 0, 2).reshape(D, D), 128)
    w1v = _slabify(f32(inputs["L1_attn_Wv"]).transpose(1, 0, 2).reshape(D, D), 512)
    w1o = _slabify(f32(inputs["L1_attn_Wo"]), 256)
    w1f = _slabify(f32(inputs["L1_ffn_weight"]).T, 128)
    bq = f32(inputs["L1_attn_bq"]).reshape(1, D)
    bk = f32(inputs["L1_attn_bk"]).reshape(1, D)
    bv = f32(inputs["L1_attn_bv"]).reshape(1, D)
    bo = f32(inputs["L1_attn_bo"]).reshape(16, 128).T.copy()
    ident = np.eye(128, dtype=np.float32)

    x = f32(inputs["L0_attn_x"])
    in_maps = []
    for c in range(8):
        b, s = c // 2, c % 2
        qb = (512 * s, 512 * s + 1024)
        xs = np.ascontiguousarray(np.concatenate(
            [x[b, qb[0]:qb[0] + 512, :], x[b, qb[1]:qb[1] + 512, :]], 0).T)
        pos = np.concatenate(
            [qb[0] + np.arange(512), qb[1] + np.arange(512)]).astype(np.float64)
        inv = (1.0 / (1e6 ** (np.arange(128, dtype=np.float64) / 128.0)))
        fr = pos[None, :] * inv[:, None]
        cosv = np.cos(fr).astype(np.float32)
        sinv = np.sin(fr).astype(np.float32)
        crow = np.zeros((1, NSLOT_MAX), dtype=np.float32)
        for i, (kind, g, kt) in enumerate(SLOTS):
            crow[0, i] = 128.0 * kt - qb[g]
        in_maps.append({
            "x_in": xs, "l0_qT": qT, "l0_kT": kT, "l0_vT": vT, "l0_oT": oT,
            "l0_qn": qn, "l0_kn": kn, "l0_w2": w2, "l0_sc": sc, "l0_sc1": sc1,
            "l1_qT": w1q, "l1_kT": w1k, "l1_vT": w1v, "l1_wo": w1o,
            "l1_bqr": bq, "l1_bkr": bk, "l1_bvr": bv, "l1_bo": bo, "l1_wT": w1f,
            "cosT": cosv, "sinT": sinv, "crowI": crow, "identI": ident,
        })
    return in_maps


def kernel(**inputs):
    nc = build_program()
    in_maps = prepare(inputs)
    res = run_bass_kernel_spmd(nc, in_maps, core_ids=list(range(8)))
    outp = np.empty((4, 2048, 2048), dtype=np.float32)
    for c in range(8):
        b, s = c // 2, c % 2
        qb = (512 * s, 512 * s + 1024)
        ot = res.results[c]["o"].T
        outp[b, qb[0]:qb[0] + 512, :] = ot[0:512]
        outp[b, qb[1]:qb[1] + 512, :] = ot[512:1024]
    return outp


if __name__ == "__main__":
    nc = build_program()
    print("built; slots:", SLOTS)
    for e in ENGS:
        pass


# version 20 (final)
# speedup vs baseline: 1.0371x; 1.0371x over previous
import math
import os
from contextlib import ExitStack

import numpy as np

import concourse.bass as bass
import concourse.mybir as mybir
from concourse.bass import DynSlice
from concourse.expressions_rust import RuntimeValue, BinaryOp
from concourse.bass_utils import run_bass_kernel_spmd

F32 = mybir.dt.float32
F32R = mybir.dt.float32r
F8 = mybir.dt.float8e4
BF16 = mybir.dt.bfloat16
ALU = mybir.AluOpType
AF = mybir.ActivationFunctionType
AX = mybir.AxisListType

D = 2048
T = 1024
NCK = 16
H0, HD0 = 8, 256
H1, HD1 = 16, 128
M0, M1 = 16.0, 8.0

ENGS = ("sync", "tensor", "vector", "scalar", "gpsimd")
DBG = os.environ.get("KDBG", "")
KSTAGE = int(os.environ.get("KSTAGE", "5"))
KH = int(os.environ.get("KH", "0"))
KDUMP = os.environ.get("KDUMP", "")


class Sem:
    def __init__(self, name, step):
        self.name, self.step, self.c, self.h = name, step, 0, None


class Reg:
    __slots__ = ("w", "r")

    def __init__(self):
        self.w, self.r = [], []


NQ = 4


class Builder:
    def __init__(self):
        self.plan = {e: [] for e in ENGS}
        self.sems = {n: Sem(n, 1) for n in ("PE", "DV", "AC", "GP", "CC")}
        for q in ("LD", "GD", "GX"):
            for i in range(NQ):
                self.sems[q + str(i)] = Sem(q + str(i), 16)
        self.qcount = {"LD": 0, "GD": 0, "GX": 0}
        self.esem = {"sync": "LD", "tensor": "PE", "vector": "DV",
                     "scalar": "AC", "gpsimd": "GP"}
        self.lastw = {e: {} for e in ENGS}

    def _wait(self, eng, sem, val):
        if self.lastw[eng].get(sem.name, -1) >= val:
            return
        self.lastw[eng][sem.name] = val
        self.plan[eng].append(lambda E, s=sem, v=val: E.wait_ge(s.h, v))

    def op(self, eng, emit, reads=(), writes=(), sem=None):
        deps = []
        for r in reads:
            deps += r.w
        for w in writes:
            deps += w.w + w.r
        best = {}
        for (s, v) in deps:
            if best.get(s.name, (None, -1))[1] < v:
                best[s.name] = (s, v)
        for (s, v) in best.values():
            self._wait(eng, s, v)
        sname = sem or self.esem[eng]
        if sname in ("LD", "GD", "GX"):
            k = self.qcount[sname]
            self.qcount[sname] += 1
            s = self.sems[sname + str(k % NQ)]
            if k >= NQ:
                self._wait(eng, s, 16 * (k // NQ))
        else:
            s = self.sems[sname]
        s.c += s.step
        tok = (s, s.c)

        def f(E, emit=emit, s=s):
            emit(E, lambda ins: ins.then_inc(s.h, s.step))
        self.plan[eng].append(f)
        for r in reads:
            r.r.append(tok)
        for w in writes:
            w.w, w.r = [tok], []
        return tok


def zones(regs, c0, c1, zsz=1024):
    return [regs[i] for i in range(c0 // zsz, min(len(regs), (c1 + zsz - 1) // zsz))]


SLOTS = []


def slot_of(c0):
    if c0 not in SLOTS:
        SLOTS.append(c0)
    return SLOTS.index(c0)


NSLOT_MAX = 64

_cache = {}


def build_program():
    if "nc" in _cache:
        return _cache["nc"]
    nc = bass.Bass("TRN2", target_bir_lowering=False, debug=False)
    B = Builder()

    di = lambda n, sh: nc.dram_tensor(n, sh, F32, kind="ExternalInput").ap()
    x_in = di("x_in", [D, T])
    l0_qT = di("l0_qT", [H0, 128, NCK * 256])
    l0_kT = di("l0_kT", [1, 128, NCK * 256])
    l0_vT = di("l0_vT", [1, 128, NCK * 256])
    l0_oT = di("l0_oT", [8, 128, NCK * 256])
    l0_qn = di("l0_qn", [128, 2])
    l0_kn = di("l0_kn", [128, 2])
    l0_w2 = di("l0_w2", [8, 128, NCK * 256])
    l0_sc = di("l0_sc", [1, 1])
    l0_sc1 = di("l0_sc1", [1, 1])
    l1_qT = di("l1_qT", [H1, 128, NCK * 128])
    l1_kT = di("l1_kT", [H1, 128, NCK * 128])
    l1_vT = di("l1_vT", [4, 128, NCK * 512])
    l1_wo = di("l1_wo", [8, 128, NCK * 256])
    l1_bqr = di("l1_bqr", [1, D])
    l1_bkr = di("l1_bkr", [1, D])
    l1_bvr = di("l1_bvr", [1, D])
    l1_bo = di("l1_bo", [128, 16])
    l1_wT = di("l1_wT", [16, 128, NCK * 128])
    cosT = di("cosT", [128, T])
    sinT = di("sinT", [128, T])
    crowI = di("crowI", [1, NSLOT_MAX])
    identI = di("identI", [128, 128])

    out = nc.dram_tensor("o", [D, T], F32, kind="ExternalOutput").ap()

    qs = nc.dram_tensor("qs", [D, T], F32, kind="Internal").ap()
    shk0 = nc.dram_tensor("shk0", [256, 2048], F32, kind="Internal", addr_space="Shared").ap()
    shv0 = nc.dram_tensor("shv0", [2048, 256], F32, kind="Internal", addr_space="Shared").ap()
    shk1 = nc.dram_tensor("shk1", [2048, 2048], F32, kind="Internal", addr_space="Shared").ap()
    shv1 = nc.dram_tensor("shv1", [2048, 2048], F32, kind="Internal", addr_space="Shared").ap()
    k1scr = nc.dram_tensor("k1scr", [2048, T], F32, kind="Internal").ap()
    bar_in = nc.dram_tensor("bar_in", [1, 8], F32, kind="Internal").ap()
    dqd = nc.dram_tensor("dqd", [16, 1024], F32, kind="Internal").ap()
    rowscr = nc.dram_tensor("rowscr", [8, 1024], F32, kind="Internal").ap()
    lscr = nc.dram_tensor("lscr", [2, 128, 4], F32, kind="Internal").ap()
    tscr = nc.dram_tensor("tscr", [2, 1024], F32, kind="Internal").ap()
    bar_out = nc.dram_tensor("bar_out", [8, 8], F32, kind="Internal").ap()

    es = ExitStack()
    sbt = lambda n, sh, dt=F32: es.enter_context(nc.sbuf_tensor(n, sh, dt))
    xbuf = sbt("xbuf", [128, NCK * T])
    pool = sbt("pool", [128, NCK * T])
    f8a = sbt("f8a", [128, NCK * T], F8)
    f8b = sbt("f8b", [128, 4096], F8)
    cost = sbt("cost", [128, T])
    sint = sbt("sint", [128, T])
    identt = sbt("identt", [128, 128])
    qnw = sbt("qnw", [128, 2])
    knw = sbt("knw", [128, 2])
    brstage = sbt("brstage", [1, 512])
    bot = sbt("bot", [128, 16])
    statr = sbt("statr", [1, 5632])
    dqrows = sbt("dqrows", [1, 1024])
    Rbuf = sbt("Rbuf", [128, 3072])
    Rtmp = sbt("Rtmp", [128, 1024])
    rtm = sbt("rtm", [128, 16])
    lsum = sbt("lsum", [128, 64])
    wtm = sbt("wtm", [128, 32])
    wtm2 = sbt("wtm2", [16, 2])
    djp = sbt("djp", [128, 512])
    cbc = sbt("cbc", [128, 112])
    crow = sbt("crow", [1, NSLOT_MAX])
    b_m0 = sbt("b_m0", [128, 1])
    b_m1 = sbt("b_m1", [128, 1])
    b_e5 = sbt("b_e5", [1, 1])
    b_e6 = sbt("b_e6", [1, 1])
    b_e6k = sbt("b_e6k", [1, 1])
    ones_p = sbt("ones_p", [128, 1])
    ones_r = sbt("ones_r", [1, 128])
    scv = sbt("scv", [1, 2])
    srsc = sbt("srsc", [1, 1])
    f4b = sbt("f4b", [128, 1])
    sc1b = sbt("sc1b", [128, 1])
    obf = sbt("obf", [128, 1024], BF16)
    ps = [es.enter_context(nc.psum_tensor("ps%d" % i, [128, 512], F32)) for i in range(8)]

    RX = [Reg() for _ in range(NCK)]
    RP = [Reg() for _ in range(NCK)]
    RF8 = [Reg() for _ in range(4)]
    RF8B = [Reg() for _ in range(2)]
    RPS = [Reg() for _ in range(8)]
    RST = [Reg() for _ in range(6)]
    RRB = [Reg() for _ in range(4)]
    RRT = Reg()
    Rrtm, Rlsum, Rscv, Rf4, Robf = Reg(), Reg(), Reg(), Reg(), Reg()
    Rdq, Rsmall, Rcbc = Reg(), Reg(), Reg()
    Rqtm, Rwtm = Reg(), Reg()
    Rwtm2, Robf2, Rwc1, Rcbc2 = Reg(), Reg(), Reg(), Reg()
    Rrow = [Reg() for _ in range(8)]
    Rlscr = [Reg(), Reg()]
    Rtscr = [Reg(), Reg()]
    Rqs, Rshk0, Rshv0, Rshk1, Rshv1, Rbar = Reg(), Reg(), Reg(), Reg(), Reg(), Reg()
    Rk1s, Rdq2 = Reg(), Reg()
    Rbrs = Reg()

    def ld(dst_ap, src_ap, regs):
        return B.op("sync", lambda E, tk, d=dst_ap, s=src_ap: tk(E.dma_start(out=d, in_=s)),
                    reads=(), writes=regs, sem="LD")

    def vop(emit, reads=(), writes=()):
        return B.op("vector", emit, reads, writes)

    def aop(emit, reads=(), writes=()):
        return B.op("scalar", emit, reads, writes)

    def gop(emit, reads=(), writes=(), sem=None):
        return B.op("gpsimd", emit, reads, writes, sem=sem)

    def top(emit, reads=(), writes=()):
        return B.op("tensor", emit, reads, writes)

    _bslot = [0]

    def bcast_dma(dst_fn, dst_regs, row_fn, row_regs, width):
        sl = _bslot[0] % 8
        _bslot[0] += 1
        B.op("gpsimd", lambda E, tk, sl=sl: tk(E.dma_start(
            out=rowscr[sl:sl + 1, 0:width], in_=row_fn()[0:1, 0:width])),
            reads=list(row_regs), writes=[Rrow[sl]], sem="GD")
        B.op("gpsimd", lambda E, tk, sl=sl: tk(E.dma_start(
            out=dst_fn()[:, 0:width],
            in_=rowscr[sl:sl + 1, 0:width].partition_broadcast(128))),
            reads=[Rrow[sl]], writes=list(dst_regs), sem="GD")

    def bcastf(dst_fn, dst_regs, row_fn, row_regs, width, bank):
        for c0 in range(0, width, 512):
            w = min(512, width - c0)
            top(lambda E, tk, c0=c0, w=w: tk(E.matmul(
                ps[bank][:, 0:w], ones_r[:], row_fn()[0:1, c0:c0 + w],
                start=True, stop=True)),
                reads=list(row_regs) + [Rsmall], writes=[RPS[bank]])
            aop(lambda E, tk, c0=c0, w=w: tk(E.copy(
                dst_fn()[:, c0:c0 + w], ps[bank][:, 0:w])),
                reads=[RPS[bank]], writes=list(dst_regs))

    def bdump(row0, src_ap, nrows):
        for s in B.sems.values():
            if s.c > 0:
                B._wait("gpsimd", s, s.c)
        B.op("gpsimd", lambda E, tk, src_ap=src_ap, row0=row0, nrows=nrows: tk(
            E.dma_start(out=out[row0:row0 + nrows, 0:src_ap.shape[-1]], in_=src_ap)),
            reads=(), writes=(), sem="GD")

    def bdump1(row0, semname):
        B.op("gpsimd", lambda E, tk, row0=row0: tk(
            E.dma_start(out=out[row0:row0 + 1, 0:1024], in_=statr[0:1, 0:1024])),
            reads=(), writes=(), sem="GD")

    for dc in range(NCK):
        ld(xbuf[:, dc * T:(dc + 1) * T], x_in[dc * 128:(dc + 1) * 128, :], [RX[dc]])
    ld(cost[:], cosT[:], [Rsmall])
    ld(sint[:], sinT[:], [Rsmall])
    ld(identt[:], identI[:], [Rsmall])
    ld(qnw[:], l0_qn[:], [Rsmall])
    ld(knw[:], l0_kn[:], [Rsmall])
    ld(bot[:], l1_bo[:], [Rsmall])
    ld(scv[:, 0:1], l0_sc[:], [Rscv])
    ld(scv[:, 1:2], l0_sc1[:], [Rscv])
    ld(crow[:], crowI[:], [Rcbc])

    vop(lambda E, tk: tk(E.memset(b_m0[:], -M0)), writes=[Rsmall])
    vop(lambda E, tk: tk(E.memset(b_m1[:], -M1)), writes=[Rsmall])
    vop(lambda E, tk: tk(E.memset(b_e5[:], 1e-5)), writes=[Rsmall])
    vop(lambda E, tk: tk(E.memset(b_e6[:], 1e-6)), writes=[Rsmall])
    vop(lambda E, tk: tk(E.memset(b_e6k[:], 256e-6)), writes=[Rsmall])
    vop(lambda E, tk: tk(E.memset(ones_p[:], 1.0)), writes=[Rsmall])
    vop(lambda E, tk: tk(E.memset(ones_r[:], 1.0)), writes=[Rsmall])
    gop(lambda E, tk: tk(E.iota(djp[:], [[1, 512]], base=0, channel_multiplier=-1,
                                allow_small_or_imprecise_dtypes=True)), writes=[Rsmall])
    bcastf(lambda: cbc, [Rcbc], lambda: crow, [Rcbc], NSLOT_MAX, 7)

    S_MU, S_MUN, S_SG, S_RI, S_HD, S_WC = 0, 1, 2, 3, 4, 5
    S_SC = S_HD
    srow = lambda z: statr[0:1, z * 1024:(z + 1) * 1024]

    def layernorm_stats():
        for qi in range(2):
            for th in range(2):
                for dc in range(NCK):
                    sqo = (dc % 2) * 1024
                    if qi == 1:
                        aop(lambda E, tk, dc=dc, th=th, sqo=sqo: tk(E.activation(
                            pool[:, sqo:sqo + 512],
                            xbuf[:, dc * T + th * 512: dc * T + (th + 1) * 512],
                            AF.Square)), reads=[RX[dc]], writes=[RP[0]] if sqo == 0 else [RP[1]])
                        rreg = RP[0] if sqo == 0 else RP[1]
                        rap = (lambda dc=dc, th=th, sqo=sqo: pool[:, sqo:sqo + 512])
                    else:
                        rreg = RX[dc]
                        rap = (lambda dc=dc, th=th:
                               xbuf[:, dc * T + th * 512: dc * T + (th + 1) * 512])
                    top(lambda E, tk, dc=dc, rap=rap: tk(E.matmul(
                        ps[6][0:1, 0:512], ones_p[:], rap(),
                        start=(dc == 0), stop=(dc == NCK - 1))),
                        reads=[rreg, Rsmall], writes=[RPS[6]])
                dz = S_HD if qi == 0 else S_MUN
                dst = statr[0:1, dz * 1024 + th * 512: dz * 1024 + (th + 1) * 512]
                aop(lambda E, tk, dst=dst, dz=dz: tk(E.copy(dst, ps[6][0:1, 0:512])),
                    reads=[RPS[6]], writes=[RST[dz]])
        sx = statr[0:1, S_HD * 1024: S_HD * 1024 + 1024]
        sq = statr[0:1, S_MUN * 1024: S_MUN * 1024 + 1024]
        RSC = [RST[S_HD], RST[S_MUN]]
        vop(lambda E, tk: tk(E.tensor_scalar(srow(S_MU), sx, 1.0 / 2048, None, op0=ALU.mult)),
            reads=RSC, writes=[RST[S_MU]])
        vop(lambda E, tk: tk(E.tensor_scalar(srow(S_SG), sq, 1.0 / 2048, None, op0=ALU.mult)),
            reads=RSC, writes=[RST[S_SG]])
        vop(lambda E, tk: tk(E.tensor_tensor(srow(S_RI), srow(S_MU), srow(S_MU), op=ALU.mult)),
            reads=[RST[S_MU]], writes=[RST[S_RI]])
        vop(lambda E, tk: tk(E.tensor_tensor(srow(S_SG), srow(S_SG), srow(S_RI), op=ALU.subtract)),
            reads=[RST[S_RI]], writes=[RST[S_SG]])
        vop(lambda E, tk: tk(E.tensor_scalar(srow(S_MUN), srow(S_MU), -1.0, None, op0=ALU.mult)),
            reads=[RST[S_MU]], writes=[RST[S_MUN]])
        aop(lambda E, tk: tk(E.activation(srow(S_SG), srow(S_SG), AF.Sqrt, bias=b_e5[:])),
            reads=[Rsmall], writes=[RST[S_SG]])
        vop(lambda E, tk: tk(E.reciprocal(srow(S_RI), srow(S_SG))),
            reads=[RST[S_SG]], writes=[RST[S_RI]])
        bcastf(lambda: Rbuf[:, 0:1024], [RRB[0]], lambda: srow(S_RI), [RST[S_RI]], 1024, 6)
        bcastf(lambda: Rbuf[:, 1024:2048], [RRB[1]], lambda: srow(S_MUN), [RST[S_MUN]], 1024, 6)
        for i in range(8):
            top(lambda E, tk, i=i: tk(E.transpose(
                ps[7][:, i:i + 1],
                statr[0:1, S_RI * 1024 + i * 128:S_RI * 1024 + (i + 1) * 128],
                identt[0:1, 0:1])), reads=[RST[S_RI], Rsmall], writes=[RPS[7]])
        aop(lambda E, tk: tk(E.copy(rtm[:, 0:8], ps[7][:, 0:8])),
            reads=[RPS[7]], writes=[Rrtm])

    def wcolsum(slab_c0, ncols, fcs, neg_sign):
        wz = zones(RP, slab_c0, slab_c0 + NCK * ncols)
        sgn = -1.0 if neg_sign else 1.0
        for ppi in range((fcs + 511) // 512):
            c0, c1 = ppi * 512, min(fcs, ppi * 512 + 512)
            for dc in range(NCK):
                top(lambda E, tk, dc=dc, c0=c0, c1=c1: tk(E.matmul(
                    ps[6][0:1, 0:c1 - c0], ones_p[:],
                    pool[:, slab_c0 + dc * ncols + c0: slab_c0 + dc * ncols + c1],
                    start=(dc == 0), stop=(dc == NCK - 1))),
                    reads=wz + [Rsmall], writes=[RPS[6]])
            vop(lambda E, tk, c0=c0, c1=c1: tk(E.tensor_scalar(
                statr[0:1, S_WC * 1024 + c0: S_WC * 1024 + c1],
                ps[6][0:1, 0:c1 - c0], sgn, None, op0=ALU.mult)),
                reads=[RPS[6]], writes=[RST[S_WC]])

    wcs = lambda c0, c1: statr[0:1, S_WC * 1024 + c0: S_WC * 1024 + c1]
    RWC = [RST[S_WC]]

    def rope6(base, rA_c, rB_c, rC_c, tqn):
        d0 = lambda: pool[:, base:base + tqn]
        d1 = lambda: pool[:, base + tqn:base + 2 * tqn]
        tA = lambda: pool[:, rA_c:rA_c + tqn]
        tB = lambda: pool[:, rB_c:rB_c + tqn]
        tC = lambda: pool[:, rC_c:rC_c + tqn]
        r0 = zones(RP, base, base + tqn)
        r1 = zones(RP, base + tqn, base + 2 * tqn)
        rA = zones(RP, rA_c, rA_c + tqn)
        rB = zones(RP, rB_c, rB_c + tqn)
        rC = zones(RP, rC_c, rC_c + tqn)
        c = lambda: cost[:, 0:tqn]
        s = lambda: sint[:, 0:tqn]
        vop(lambda E, tk: tk(E.tensor_tensor(tA(), c(), d1(), op=ALU.mult)),
            reads=r1 + [Rsmall], writes=rA)
        vop(lambda E, tk: tk(E.tensor_tensor(tB(), s(), d1(), op=ALU.mult)),
            reads=r1 + [Rsmall], writes=rB)
        vop(lambda E, tk: tk(E.tensor_tensor(tC(), s(), d0(), op=ALU.mult)),
            reads=r0 + [Rsmall], writes=rC)
        vop(lambda E, tk: tk(E.tensor_tensor(d1(), tC(), tA(), op=ALU.add)),
            reads=rA + rC, writes=r1)
        vop(lambda E, tk: tk(E.tensor_tensor(tC(), c(), d0(), op=ALU.mult)),
            reads=r0 + [Rsmall], writes=rC)
        vop(lambda E, tk: tk(E.tensor_tensor(d0(), tC(), tB(), op=ALU.subtract)),
            reads=rB + rC, writes=r0)

    WS0, WS1 = 0, 4096
    KST, VST = 8192, 10240
    QH, TMPA, TMPB = 12288, 14336, 15360

    _srv_cache = {}

    def get_srv(E, mulv):
        if mulv not in _srv_cache:
            if "pid" not in _srv_cache:
                pid = E.partition_id()
                _srv_cache["pid"] = RuntimeValue(
                    BinaryOp(pid, RuntimeValue(1), ALU.bitwise_and), 0, 1)
            _srv_cache[mulv] = RuntimeValue(
                BinaryOp(_srv_cache["pid"], RuntimeValue(mulv), ALU.mult), 0, mulv)
        return _srv_cache[mulv]

    def get_qb(E, g):
        if g == 0:
            return get_srv(E, 512)
        key = ("qb", 1)
        if key not in _srv_cache:
            _srv_cache[key] = RuntimeValue(
                BinaryOp(get_srv(E, 512), RuntimeValue(1024), ALU.add), 1024, 1536)
        return _srv_cache[key]

    def head_rms(base, eps_bias, extra_scale_pow2):
        for ch in range(2):
            aop(lambda E, tk, ch=ch: tk(E.activation(
                pool[:, TMPA:TMPA + 1024], pool[:, base + ch * 1024:base + (ch + 1) * 1024],
                AF.Square)), reads=zones(RP, base + ch * 1024, base + (ch + 1) * 1024),
                writes=zones(RP, TMPA, TMPA + 1024))
            for th in range(2):
                bank = 6 if th == 0 else 7
                top(lambda E, tk, th=th, ch=ch, bank=bank: tk(E.matmul(
                    ps[bank][0:1, 0:512], ones_p[:],
                    pool[:, TMPA + th * 512:TMPA + (th + 1) * 512],
                    start=(ch == 0), stop=(ch == 1))),
                    reads=zones(RP, TMPA, TMPA + 1024) + [Rsmall], writes=[RPS[bank]])
        for th in range(2):
            bank = 6 if th == 0 else 7
            aop(lambda E, tk, th=th, bank=bank: tk(E.copy(
                statr[0:1, S_HD * 1024 + th * 512:S_HD * 1024 + (th + 1) * 512],
                ps[bank][0:1, 0:512])), reads=[RPS[bank]], writes=[RST[S_HD]])
        aop(lambda E, tk: tk(E.activation(srow(S_HD), srow(S_HD), AF.Sqrt,
                                          bias=eps_bias[:], scale=extra_scale_pow2)),
            reads=[Rsmall], writes=[RST[S_HD]])
        vop(lambda E, tk: tk(E.reciprocal(srow(S_HD), srow(S_HD))), writes=[RST[S_HD]])

    def bcast_hd_mul(base, nchunks, width=1024):
        bcastf(lambda: Rtmp[:, 0:width], [RRT], lambda: srow(S_HD)[:, 0:width],
               [RST[S_HD]], width, 6)
        for ch in range(nchunks):
            vop(lambda E, tk, ch=ch: tk(E.tensor_tensor(
                pool[:, base + ch * width:base + (ch + 1) * width],
                pool[:, base + ch * width:base + (ch + 1) * width],
                Rtmp[:, 0:width], op=ALU.mult)),
                reads=[RRT], writes=zones(RP, base + ch * width, base + (ch + 1) * width))

    if KSTAGE >= 2:
        layernorm_stats()
        if KDUMP == 'sent':
            bdump(1025, statr[0:1, 0:1024], 1)
        if KDUMP == "st":
            for zz in range(5):
                B.op("gpsimd", lambda E, tk, zz=zz: tk(E.dma_start(
                    out=out[zz:zz + 1, :], in_=statr[0:1, zz * 1024:(zz + 1) * 1024])),
                    reads=[RST[i] for i in range(6)], writes=(), sem="GD")
            for rr in range(3):
                B.op("gpsimd", lambda E, tk, rr=rr: tk(E.dma_start(
                    out=out[128 * (rr + 1):128 * (rr + 2), :],
                    in_=Rbuf[:, rr * 1024:(rr + 1) * 1024])),
                    reads=[RRB[0], RRB[1], RRB[2]], writes=(), sem="GD")
            B.op("gpsimd", lambda E, tk: tk(E.dma_start(
                out=out[520:521, :], in_=rtm[:, 0:8].transpose() if False else statr[0:1, 0:1024])),
                reads=[Rrtm], writes=(), sem="GD")

        ld(pool[:, WS0:WS0 + 4096], l0_kT[0], zones(RP, WS0, WS0 + 4096))
        wcolsum(WS0, 256, 256, True)
        for ch in range(2):
            for th in range(2):
                bank = ch * 2 + th
                def emit(E, tk, ch=ch, th=th, bank=bank):
                    for dc in range(NCK):
                        E.matmul(ps[bank][:, 0:512],
                                 pool[:, WS0 + dc * 256 + ch * 128: WS0 + dc * 256 + ch * 128 + 128],
                                 xbuf[:, dc * T + th * 512: dc * T + (th + 1) * 512],
                                 start=(dc == 0), stop=False)
                    tk(E.matmul(ps[bank][:, 0:512],
                                wcs(ch * 128, ch * 128 + 128),
                                srow(S_MU)[:, th * 512:(th + 1) * 512],
                                start=False, stop=True))
                top(emit, reads=[RX[dc] for dc in range(NCK)] + zones(RP, WS0, WS0 + 4096)
                    + RWC + [RST[S_MU]], writes=[RPS[bank]])
                vop(lambda E, tk, b=bank, ch=ch, th=th: tk(E.tensor_tensor(
                    pool[:, KST + ch * 1024 + th * 512: KST + ch * 1024 + (th + 1) * 512],
                    ps[b][:, 0:512], Rbuf[:, th * 512:(th + 1) * 512], op=ALU.mult)),
                    reads=[RPS[bank], RRB[0]],
                    writes=zones(RP, KST + ch * 1024 + th * 512, KST + ch * 1024 + (th + 1) * 512))
        head_rms(KST, b_e6k, 1.0)
        for ch in range(2):
            vop(lambda E, tk, ch=ch: tk(E.tensor_scalar(
                pool[:, KST + ch * 1024:KST + (ch + 1) * 1024],
                pool[:, KST + ch * 1024:KST + (ch + 1) * 1024],
                knw[:, ch:ch + 1], None, op0=ALU.mult)),
                reads=[Rsmall], writes=zones(RP, KST + ch * 1024, KST + (ch + 1) * 1024))
        rope6(KST, VST, TMPA, TMPB, 1024)
        bcast_hd_mul(KST, 2)
        for sg in range(2):
            for sch in range(2):
                def store_shk0(E, tk, sg=sg, sch=sch):
                    rv = get_qb(E, sg)
                    c0 = KST + sch * 1024 + sg * 512
                    tk(E.dma_start(
                        out=shk0[sch * 128:(sch + 1) * 128, DynSlice(rv, 512)],
                        in_=pool[:, c0:c0 + 512]))
                gop(store_shk0, reads=zones(RP, KST + sch * 1024 + sg * 512,
                                            KST + sch * 1024 + sg * 512 + 512),
                    writes=[Rshk0], sem="GX")
        if KDUMP == 'sent':
            bdump(1026, statr[0:1, 0:1024], 1)
        if KDUMP == "kv":
            B.op("gpsimd", lambda E, tk: tk(E.dma_start(
                out=out[0:256, :].rearrange("(c p) t -> p c t", p=128),
                in_=pool[:, KST:KST + 2048].rearrange("p (c t) -> p c t", c=2))),
                reads=zones(RP, KST, KST + 2048), writes=(), sem="GD")

        ld(pool[:, WS1:WS1 + 4096], l0_vT[0], zones(RP, WS1, WS1 + 4096))
        wcolsum(WS1 , 256, 256, True)
        for tt in range(8):
            bank = tt % 4
            def emitv(E, tk, tt=tt, bank=bank):
                for dc in range(NCK):
                    E.matmul(ps[bank][:, 0:256],
                             xbuf[:, dc * T + tt * 128: dc * T + (tt + 1) * 128],
                             pool[:, WS1 + dc * 256: WS1 + (dc + 1) * 256],
                             start=(dc == 0), stop=False)
                tk(E.matmul(ps[bank][:, 0:256],
                            statr[0:1, S_MU * 1024 + tt * 128:S_MU * 1024 + (tt + 1) * 128],
                            wcs(0, 256), start=False, stop=True))
            top(emitv, reads=[RX[dc] for dc in range(NCK)] + zones(RP, WS1, WS1 + 4096)
                + RWC + [RST[S_MU]], writes=[RPS[bank]])
            vop(lambda E, tk, tt=tt, bank=bank: tk(E.tensor_scalar(
                pool[:, VST + tt * 256: VST + (tt + 1) * 256], ps[bank][:, 0:256],
                rtm[:, tt:tt + 1], None, op0=ALU.mult)),
                reads=[RPS[bank], Rrtm],
                writes=zones(RP, VST + tt * 256, VST + (tt + 1) * 256))
        for sg in range(2):
            def store_shv0(E, tk, sg=sg):
                rv = get_qb(E, sg)
                src = pool[:, VST + sg * 1024:VST + (sg + 1) * 1024].rearrange(
                    "p (t j) -> p t j", t=4)
                dst = shv0[DynSlice(rv, 512), :].rearrange("(t p) j -> p t j", p=128)
                tk(E.dma_start(out=dst, in_=src))
            gop(store_shv0, reads=zones(RP, VST + sg * 1024, VST + (sg + 1) * 1024),
                writes=[Rshv0], sem="GX")
        if KDUMP == 'sem':
            bdump1(1030, 'LD')
            bdump1(1031, 'GD')
            bdump1(1032, 'GX')
            bdump1(1033, 'PE')
            bdump1(1034, 'DV')
            bdump1(1035, 'AC')
            bdump1(1036, 'GP')
        if KDUMP == 'sent':
            bdump(1027, statr[0:1, 0:1024], 1)
        if KDUMP == "kv":
            B.op("gpsimd", lambda E, tk: tk(E.dma_start(
                out=out[256:512, :].rearrange("(c p) t -> p c t", p=128),
                in_=pool[:, VST:VST + 2048].rearrange("p (c t) -> p c t", c=2))),
                reads=zones(RP, VST, VST + 2048), writes=(), sem="GD")

        for h in range(KH or H0):
            wsb = WS0 if h % 2 == 0 else WS1
            wz = zones(RP, wsb, wsb + 4096)
            ld(pool[:, wsb:wsb + 4096], l0_qT[h], wz)
            if KDUMP == "w" and h == 0:
                bdump(0, pool[:, wsb:wsb + 1024], 128)
                bdump(256, cost[:, 0:1024], 128)
                bdump(384, identt[:], 128)
                bdump(512, crow[:], 1)
                bdump(513, qnw[0:1, :], 1)
                bdump(640, xbuf[:, 0:1024], 128)
            wcolsum(wsb, 256, 256, True)
            for ch in range(2):
                for th in range(2):
                    bank = ch * 2 + th
                    def emitq(E, tk, ch=ch, th=th, bank=bank, wsb=wsb):
                        for dc in range(NCK):
                            E.matmul(ps[bank][:, 0:512],
                                     pool[:, wsb + dc * 256 + ch * 128: wsb + dc * 256 + ch * 128 + 128],
                                     xbuf[:, dc * T + th * 512: dc * T + (th + 1) * 512],
                                     start=(dc == 0), stop=False)
                        tk(E.matmul(ps[bank][:, 0:512],
                                    wcs(ch * 128, ch * 128 + 128),
                                    srow(S_MU)[:, th * 512:(th + 1) * 512],
                                    start=False, stop=True))
                    top(emitq, reads=[RX[dc] for dc in range(NCK)] + wz + RWC + [RST[S_MU]],
                        writes=[RPS[bank]])
                    vop(lambda E, tk, b=bank, ch=ch, th=th: tk(E.tensor_tensor(
                        pool[:, QH + ch * 1024 + th * 512: QH + ch * 1024 + (th + 1) * 512],
                        ps[b][:, 0:512], Rbuf[:, th * 512:(th + 1) * 512], op=ALU.mult)),
                        reads=[RPS[bank], RRB[0]],
                        writes=zones(RP, QH + ch * 1024 + th * 512, QH + ch * 1024 + (th + 1) * 512))
            head_rms(QH, b_e6, 1.0 / 256)
            for ch in range(2):
                vop(lambda E, tk, ch=ch: tk(E.tensor_scalar(
                    pool[:, QH + ch * 1024:QH + (ch + 1) * 1024],
                    pool[:, QH + ch * 1024:QH + (ch + 1) * 1024],
                    qnw[:, ch:ch + 1], None, op0=ALU.mult)),
                    reads=[Rsmall], writes=zones(RP, QH + ch * 1024, QH + (ch + 1) * 1024))
            rope6(QH, KST, TMPA, TMPB, 1024)
            bcast_hd_mul(QH, 2)
            B.op("gpsimd", lambda E, tk, h=h: tk(E.dma_start(
                out=qs[h * 256:(h + 1) * 256, :].rearrange("(c p) t -> p c t", p=128),
                in_=pool[:, QH:QH + 2048].rearrange("p (c t) -> p c t", c=2))),
                reads=zones(RP, QH, QH + 2048), writes=[Rqs], sem="GD")
            if KDUMP == "q" and h == 0:
                for ch in range(2):
                    B.op("gpsimd", lambda E, tk, ch=ch: tk(E.dma_start(
                        out=out[ch * 128:(ch + 1) * 128, :],
                        in_=pool[:, QH + ch * 1024:QH + (ch + 1) * 1024])),
                        reads=zones(RP, QH, QH + 2048), writes=(), sem="GD")

        def barrier(kv_regs):
            gop(lambda E, tk: tk(E.dma_start(out=bar_in[:], in_=statr[0:1, 0:8])),
                reads=[RST[S_MU]], writes=[Rbar], sem="GD")
            return B.op("gpsimd", lambda E, tk: tk(E.collective_compute(
                "AllGather", ALU.bypass, replica_groups=[list(range(8))],
                ins=[bar_in.opt()], outs=[bar_out.opt()])),
                reads=[Rbar], writes=[Rbar] + kv_regs, sem="CC")

        barrier([Rshk0, Rshv0])

        CTX = 0
        KB = [8192, 9216]
        VB = [10240, 11264]
        QHH = 12288
        PT = [13312, 13824]
        SG = 14336
        MT = 14848

        def attention(layer):
            if os.environ.get("KTRACE"):
                print("ATTENTION CALLED", layer)
            H = H0 if layer == 0 else H1
            HDC = 2 if layer == 0 else 1
            HD = HD0 if layer == 0 else HD1
            mb = b_m0 if layer == 0 else b_m1
            ksrc, vsrc = (shk0, shv0) if layer == 0 else (shk1, shv1)
            krg, vrg = (Rshk0, Rshv0) if layer == 0 else (Rshk1, Rshv1)
            for g in range(2):
                nblk = (2, 4)[g]
                ntk = (8, 16)[g]

                def head_tiles(h, g=g, nblk=nblk, ntk=ntk):
                    hp = h % 2
                    QHHp = QHH if hp == 0 else 15360
                    if layer == 0:
                        qsl = qs[h * 256:(h + 1) * 256, g * 512:(g + 1) * 512].rearrange(
                            "(c p) t -> p c t", p=128)
                        qdst = pool[:, QHHp:QHHp + 1024].rearrange("p (c t) -> p c t", c=2)
                    else:
                        qsl = qs[h * 128:(h + 1) * 128, g * 512:(g + 1) * 512]
                        qdst = pool[:, QHHp:QHHp + 512]
                    ld(qdst, qsl, zones(RP, QHHp, QHHp + HDC * 512))
                    qz = zones(RP, QHHp, QHHp + HDC * 512)
                    ctx_banks = ([4, 5] if hp == 0 else [6, 7]) if HDC == 2 else [4 + hp]
                    first_ctx = True
                    lparts = {i: [] for i in range(4)}
                    for kb in range(nblk):
                        kbr = kb % 2
                        if layer == 0:
                            kap = ksrc[:, kb * 512:(kb + 1) * 512].rearrange(
                                "(c p) t -> p c t", p=128)
                            kdst = pool[:, KB[kbr]:KB[kbr] + 1024].rearrange(
                                "p (c t) -> p c t", c=2)
                            vap = vsrc[kb * 512:(kb + 1) * 512, :].rearrange(
                                "(t p) j -> p t j", p=128)
                            vdst = pool[:, VB[kbr]:VB[kbr] + 1024].rearrange(
                                "p (t j) -> p t j", t=4)
                        else:
                            kap = ksrc[h * 128:(h + 1) * 128, kb * 512:(kb + 1) * 512]
                            kdst = pool[:, KB[kbr]:KB[kbr] + 512]
                            vap = vsrc[kb * 512:(kb + 1) * 512, h * 128:(h + 1) * 128].rearrange(
                                "(t p) j -> p t j", p=128)
                            vdst = pool[:, VB[kbr]:VB[kbr] + 512].rearrange(
                                "p (t j) -> p t j", t=4)
                        kz = zones(RP, KB[kbr], KB[kbr] + 1024)
                        vz = zones(RP, VB[kbr], VB[kbr] + 1024)
                        B.op("sync", lambda E, tk, d=kdst, s=kap: tk(E.dma_start(out=d, in_=s)),
                             reads=[krg], writes=kz, sem="LD")
                        B.op("sync", lambda E, tk, d=vdst, s=vap: tk(E.dma_start(out=d, in_=s)),
                             reads=[vrg], writes=vz, sem="LD")
                        for tkt in range(4):
                            kt = kb * 4 + tkt
                            if kt >= ntk:
                                continue
                            tbank = 2 + (kt % 2)
                            def emitT(E, tk, tkt=tkt, kbr=kbr, tbank=tbank, QHHp=QHHp):
                                for ch in range(HDC):
                                    ins = E.matmul(
                                        ps[tbank][:, 0:512],
                                        pool[:, KB[kbr] + ch * 512 + tkt * 128:
                                             KB[kbr] + ch * 512 + tkt * 128 + 128],
                                        pool[:, QHHp + ch * 512:QHHp + (ch + 1) * 512],
                                        start=(ch == 0), stop=(ch == HDC - 1))
                                tk(ins)
                            top(emitT, reads=qz + kz, writes=[RPS[tbank]])
                            ptr = kt % 2
                            aop(lambda E, tk, tbank=tbank, ptr=ptr: tk(E.activation(
                                pool[:, PT[ptr]:PT[ptr] + 512], ps[tbank][:, 0:512], AF.Exp,
                                bias=mb[:])),
                                reads=[RPS[tbank], Rsmall],
                                writes=zones(RP, PT[ptr], PT[ptr] + 512))
                            minqb = (0, 1024)[g]
                            need_mask = not (minqb - 128 * kt >= 127)
                            if need_mask:
                                sl = slot_of(("T", g, kt))
                                vop(lambda E, tk, sl=sl, ptr=ptr: tk(E.scalar_tensor_tensor(
                                    pool[:, PT[ptr]:PT[ptr] + 512], djp[:],
                                    cbc[:, sl:sl + 1], pool[:, PT[ptr]:PT[ptr] + 512],
                                    op0=ALU.is_ge, op1=ALU.mult)),
                                    reads=[Rcbc, Rsmall],
                                    writes=zones(RP, PT[ptr], PT[ptr] + 512))
                            pt_c, pt_z = PT[ptr], zones(RP, PT[ptr], PT[ptr] + 512)
                            for ci, cbank in enumerate(ctx_banks):
                                def emitC(E, tk, ci=ci, cbank=cbank, tkt=tkt, kbr=kbr,
                                          pt_c=pt_c, first=first_ctx, last=(kt == ntk - 1)):
                                    ins = E.matmul(
                                        ps[cbank][:, 0:512],
                                        pool[:, VB[kbr] + tkt * HD + ci * 128:
                                             VB[kbr] + tkt * HD + ci * 128 + 128],
                                        pool[:, pt_c:pt_c + 512],
                                        start=first, stop=last)
                                    tk(ins)
                                top(emitC, reads=vz + pt_z, writes=[RPS[cbank]])
                            acc_c = SG if hp == 0 else MT
                            acc_z = zones(RP, acc_c, acc_c + 512)
                            if first_ctx:
                                vop(lambda E, tk, pt_c=pt_c, acc_c=acc_c: tk(E.tensor_copy(
                                    pool[:, acc_c:acc_c + 512], pool[:, pt_c:pt_c + 512])),
                                    reads=pt_z, writes=acc_z)
                            else:
                                vop(lambda E, tk, pt_c=pt_c, acc_c=acc_c: tk(E.tensor_tensor(
                                    pool[:, acc_c:acc_c + 512], pool[:, acc_c:acc_c + 512],
                                    pool[:, pt_c:pt_c + 512], op=ALU.add)),
                                    reads=pt_z + acc_z, writes=acc_z)
                            first_ctx = False

                def head_evac(h, g=g):
                    hp = h % 2
                    hz = S_HD if hp == 0 else S_SG
                    HR = RST[S_HD] if hp == 0 else RST[S_SG]
                    RRTp = RRT if hp == 0 else Rcbc2
                    acc_c = SG if hp == 0 else MT
                    acc_z = zones(RP, acc_c, acc_c + 512)
                    ctx_banks = ([4, 5] if hp == 0 else [6, 7]) if HDC == 2 else [4 + hp]
                    top(lambda E, tk, hp=hp, acc_c=acc_c: tk(E.matmul(
                        ps[hp][0:1, 0:512], ones_p[:], pool[:, acc_c:acc_c + 512],
                        start=True, stop=True)),
                        reads=acc_z + [Rsmall], writes=[RPS[hp]])
                    aop(lambda E, tk, hz=hz, hp=hp: tk(E.copy(
                        statr[0:1, hz * 1024:hz * 1024 + 512], ps[hp][0:1, 0:512])),
                        reads=[RPS[hp]], writes=[HR])
                    vop(lambda E, tk, hz=hz: tk(E.reciprocal(
                        statr[0:1, hz * 1024 + 512:hz * 1024 + 1024],
                        statr[0:1, hz * 1024:hz * 1024 + 512])),
                        reads=[HR], writes=[HR])
                    bcastf(lambda hp=hp: Rtmp[:, hp * 512:hp * 512 + 512], [RRTp],
                           lambda hz=hz: statr[0:1, hz * 1024 + 512:hz * 1024 + 1024],
                           [HR], 512, hp)
                    for ci, cbank in enumerate(ctx_banks):
                        cg = h * HDC + ci
                        vop(lambda E, tk, cbank=cbank, cg=cg, hp=hp: tk(E.tensor_tensor(
                            pool[:, CTX + cg * 512: CTX + (cg + 1) * 512],
                            ps[cbank][:, 0:512], Rtmp[:, hp * 512:hp * 512 + 512],
                            op=ALU.mult)),
                            reads=[RPS[cbank], RRTp],
                            writes=zones(RP, CTX + cg * 512, CTX + (cg + 1) * 512))
                    if KDUMP == "att" and h == 0 and layer == 0:
                        bdump(1280 + g * 128, pool[:, CTX:CTX + 512], 128)
                        bdump(1536 + g * 128, lsum[:, 0:48], 128)
                        bdump(1792 + g, statr[0:1, S_HD * 1024:S_HD * 1024 + 1024], 1)
                        if g == 0:
                            bdump(1800, djp[:], 128)
                            bdump(1930, cbc[0:2, 0:64], 2)

                NH = KH or H
                for h in range(NH):
                    head_tiles(h)
                    if h >= 1:
                        head_evac(h - 1)
                head_evac(NH - 1)
                if KDUMP == "l1" and layer == 1 and g == 0:
                    for ii in range(8):
                        bdump(768 + ii * 128, pool[:, CTX + ii * 1024:CTX + (ii + 1) * 1024], 128)
                if KDUMP == "ctx" and layer == 0 and g == 0:
                    for ii in range(8):
                        bdump(ii * 128, pool[:, CTX + ii * 1024:CTX + (ii + 1) * 1024], 128)
                OW = [8192, 12288]
                wsrc = l0_oT if layer == 0 else l1_wo
                for sl8 in range(8):
                    owb = OW[sl8 % 2]
                    owz = zones(RP, owb, owb + 4096)
                    ld(pool[:, owb:owb + 4096], wsrc[sl8], owz)
                    for fc2 in range(2):
                        fglob = sl8 * 2 + fc2
                        bank = fc2 % 2
                        def emitO(E, tk, owb=owb, fc2=fc2, bank=bank):
                            for dc in range(NCK):
                                ins = E.matmul(
                                    ps[bank][:, 0:512],
                                    pool[:, owb + dc * 256 + fc2 * 128:
                                         owb + dc * 256 + fc2 * 128 + 128],
                                    pool[:, CTX + dc * 512:CTX + (dc + 1) * 512],
                                    start=(dc == 0), stop=(dc == NCK - 1))
                            tk(ins)
                        top(emitO, reads=owz + zones(RP, CTX, CTX + 8192), writes=[RPS[bank]])
                        xsl = lambda fglob=fglob: xbuf[:, fglob * T + g * 512: fglob * T + g * 512 + 512]
                        if layer == 0:
                            vop(lambda E, tk, bank=bank, fglob=fglob, g=g: tk(E.tensor_tensor(
                                xbuf[:, fglob * T + g * 512: fglob * T + g * 512 + 512],
                                ps[bank][:, 0:512],
                                xbuf[:, fglob * T + g * 512: fglob * T + g * 512 + 512],
                                op=ALU.add)),
                                reads=[RPS[bank], RX[fglob]], writes=[RX[fglob]])
                        else:
                            vop(lambda E, tk, bank=bank, fglob=fglob, g=g: tk(E.scalar_tensor_tensor(
                                xbuf[:, fglob * T + g * 512: fglob * T + g * 512 + 512],
                                ps[bank][:, 0:512], bot[:, fglob:fglob + 1],
                                xbuf[:, fglob * T + g * 512: fglob * T + g * 512 + 512],
                                op0=ALU.add, op1=ALU.add)),
                                reads=[RPS[bank], RX[fglob], Rsmall], writes=[RX[fglob]])

        attention(0)

    if KSTAGE >= 3:
        layernorm_stats()
        vop(lambda E, tk: tk(E.tensor_scalar(srow(S_HD), srow(S_RI), scv[0:1, 0:1], None,
                                             op0=ALU.mult)),
            reads=[RST[S_RI], Rscv], writes=[RST[S_HD]])
        bcastf(lambda: Rbuf[:, 2048:3072], [RRB[2]], lambda: srow(S_HD), [RST[S_HD]], 1024, 6)
        vop(lambda E, tk: tk(E.reciprocal(statr[0:1, S_HD * 1024:S_HD * 1024 + 1],
                                          scv[0:1, 0:1])),
            reads=[Rscv], writes=[RST[S_HD]])
        vop(lambda E, tk: tk(E.reciprocal(statr[0:1, S_HD * 1024 + 1:S_HD * 1024 + 2],
                                          scv[0:1, 1:2])),
            reads=[Rscv], writes=[RST[S_HD]])
        vop(lambda E, tk: tk(E.tensor_tensor(statr[0:1, S_HD * 1024:S_HD * 1024 + 1],
                                             statr[0:1, S_HD * 1024:S_HD * 1024 + 1],
                                             statr[0:1, S_HD * 1024 + 1:S_HD * 1024 + 2],
                                             op=ALU.mult)),
            reads=[RST[S_HD]], writes=[RST[S_HD]])
        bcastf(lambda: f4b, [Rf4], lambda: statr[0:1, S_HD * 1024:S_HD * 1024 + 1],
               [RST[S_HD]], 1, 6)
        bcastf(lambda: sc1b, [Rf4], lambda: scv[0:1, 1:2], [Rscv], 1, 6)
        TQ0, TQ1 = 8192, 9216
        for dc in range(NCK):
            tq = TQ0 if dc % 2 == 0 else TQ1
            vop(lambda E, tk, dc=dc, tq=tq: tk(E.tensor_tensor(
                pool[:, tq:tq + 1024], xbuf[:, dc * T:(dc + 1) * T], Rbuf[:, 1024:2048],
                op=ALU.add)), reads=[RX[dc], RRB[1]], writes=zones(RP, tq, tq + 1024))
            vop(lambda E, tk, dc=dc, tq=tq: tk(E.tensor_tensor(
                f8a[:, dc * T:(dc + 1) * T], pool[:, tq:tq + 1024], Rbuf[:, 2048:3072],
                op=ALU.mult)), reads=zones(RP, tq, tq + 1024) + [RRB[2]],
                writes=[RF8[dc // 4]])
        for sl8 in range(8):
            wb = WS0 if sl8 % 2 == 0 else WS1
            wz = zones(RP, wb, wb + 4096)
            f8r = 0
            ld(pool[:, wb:wb + 4096], l0_w2[sl8], wz)
            vop(lambda E, tk, wb=wb: tk(E.tensor_scalar(
                f8b[:, 0:4096], pool[:, wb:wb + 4096],
                sc1b[:], None, op0=ALU.mult)),
                reads=wz + [Rf4], writes=[RF8B[0]])
            for nc2 in range(2):
                fglob = sl8 * 2 + nc2
                for th in range(2):
                    bank = nc2 * 2 + th
                    def emitF(E, tk, f8r=f8r, nc2=nc2, th=th, bank=bank):
                        for kc in range(NCK):
                            ins = E.matmul(
                                ps[bank][:, 0:512],
                                f8b[:, f8r * 4096 + kc * 256 + nc2 * 128:
                                    f8r * 4096 + kc * 256 + nc2 * 128 + 128],
                                f8a[:, kc * T + th * 512: kc * T + th * 512 + 512],
                                start=(kc == 0), stop=(kc == NCK - 1))
                        tk(ins)
                    top(emitF, reads=[RF8B[f8r]] + RF8, writes=[RPS[bank]])
                    vop(lambda E, tk, bank=bank, fglob=fglob, th=th: tk(E.scalar_tensor_tensor(
                        xbuf[:, fglob * T + th * 512: fglob * T + th * 512 + 512],
                        ps[bank][:, 0:512], f4b[:],
                        xbuf[:, fglob * T + th * 512: fglob * T + th * 512 + 512],
                        op0=ALU.mult, op1=ALU.add)),
                        reads=[RPS[bank], RX[fglob], Rf4], writes=[RX[fglob]])

    if KSTAGE >= 4:
        layernorm_stats()
        vop(lambda E, tk: tk(E.tensor_scalar(srow(S_HD), srow(S_RI), 1.0 / math.sqrt(128.0),
                                             None, op0=ALU.mult)),
            reads=[RST[S_RI]], writes=[RST[S_HD]])
        bcastf(lambda: Rbuf[:, 2048:3072], [RRB[2]], lambda: srow(S_HD), [RST[S_HD]], 1024, 6)

        for h in range(KH or H1):
            for which in range(2):
                wsrc = l1_qT if which == 0 else l1_kT
                brsrc = l1_bqr if which == 0 else l1_bkr
                wsb = WS0 if (h * 2 + which) % 2 == 0 else WS1
                wz = zones(RP, wsb, wsb + 2048)
                ld(pool[:, wsb:wsb + 2048], wsrc[h], wz)
                B.op("sync", lambda E, tk, h=h, brsrc=brsrc: tk(E.dma_start(
                    out=brstage[0:1, 0:128], in_=brsrc[0:1, h * 128:(h + 1) * 128])),
                    reads=(), writes=[Rbrs], sem="LD")
                wcolsum(wsb, 128, 128, True)
                for th in range(2):
                    bank = th
                    def emitqk(E, tk, th=th, bank=bank, wsb=wsb):
                        for dc in range(NCK):
                            E.matmul(ps[bank][:, 0:512],
                                     pool[:, wsb + dc * 128: wsb + dc * 128 + 128],
                                     xbuf[:, dc * T + th * 512: dc * T + (th + 1) * 512],
                                     start=(dc == 0), stop=False)
                        E.matmul(ps[bank][:, 0:512],
                                 wcs(0, 128),
                                 srow(S_MU)[:, th * 512:(th + 1) * 512],
                                 start=False, stop=False)
                        tk(E.matmul(ps[bank][:, 0:512],
                                    brstage[0:1, 0:128],
                                    srow(S_SG)[:, th * 512:(th + 1) * 512],
                                    start=False, stop=True))
                    top(emitqk, reads=[RX[dc] for dc in range(NCK)] + wz + RWC
                        + [RST[S_MU], RST[S_SG], Rbrs], writes=[RPS[bank]])
                    rb = 2 if which == 0 else 0
                    koff = VST if which == 0 else KST + (h % 2) * 1024
                    vop(lambda E, tk, bank=bank, th=th, rb=rb, koff=koff: tk(E.tensor_tensor(
                        pool[:, koff + th * 512: koff + (th + 1) * 512],
                        ps[bank][:, 0:512], Rbuf[:, rb * 1024 + th * 512: rb * 1024 + (th + 1) * 512],
                        op=ALU.mult)),
                        reads=[RPS[bank], RRB[rb]],
                        writes=zones(RP, koff + th * 512, koff + (th + 1) * 512))
                if which == 0:
                    if KDUMP == "l1" and h == 0:
                        bdump(0, pool[:, VST:VST + 1024], 128)
                    B.op("gpsimd", lambda E, tk, h=h: tk(E.dma_start(
                        out=qs[h * 128:(h + 1) * 128, :], in_=pool[:, VST:VST + 1024])),
                        reads=zones(RP, VST, VST + 1024), writes=[Rqs], sem="GD")
                elif h % 2 == 1:
                    if KDUMP == "l1" and h == 1:
                        bdump(128, pool[:, KST:KST + 1024], 128)
                    def store_k1(E, tk, h=h):
                        tk(E.dma_start(
                            out=k1scr[(h - 1) * 128:(h + 1) * 128, :].rearrange(
                                "(c p) t -> p c t", p=128),
                            in_=pool[:, KST:KST + 2048].rearrange("p (c t) -> p c t", c=2)))
                    gop(store_k1, reads=zones(RP, KST, KST + 2048), writes=[Rk1s], sem="GX")

        for sg in range(2):
            def k1_bulk(E, tk, sg=sg):
                rv = get_qb(E, sg)
                tk(E.dma_start(out=shk1[:, DynSlice(rv, 512)],
                               in_=k1scr[:, sg * 512:(sg + 1) * 512]))
            gop(k1_bulk, reads=[Rk1s], writes=[Rshk1], sem="GX")

        for vs in range(4):
            wsb = WS0
            wz = zones(RP, wsb, wsb + 8192)
            ld(pool[:, wsb:wsb + 8192], l1_vT[vs], wz)
            B.op("sync", lambda E, tk, vs=vs: tk(E.dma_start(
                out=brstage[0:1, 0:512], in_=l1_bvr[0:1, vs * 512:(vs + 1) * 512])),
                reads=(), writes=[Rbrs], sem="LD")
            wcolsum(wsb, 512, 512, True)
            for tt in range(8):
                bank = tt % 4
                def emitv1(E, tk, tt=tt, bank=bank, wsb=wsb, vs=vs):
                    for dc in range(NCK):
                        E.matmul(ps[bank][:, 0:512],
                                 xbuf[:, dc * T + tt * 128: dc * T + (tt + 1) * 128],
                                 pool[:, wsb + dc * 512: wsb + (dc + 1) * 512],
                                 start=(dc == 0), stop=False)
                    E.matmul(ps[bank][:, 0:512],
                             statr[0:1, S_MU * 1024 + tt * 128:S_MU * 1024 + (tt + 1) * 128],
                             wcs(0, 512), start=False, stop=False)
                    tk(E.matmul(ps[bank][:, 0:512],
                                statr[0:1, S_SG * 1024 + tt * 128:S_SG * 1024 + (tt + 1) * 128],
                                brstage[0:1, 0:512],
                                start=False, stop=True))
                top(emitv1, reads=[RX[dc] for dc in range(NCK)] + wz + RWC
                    + [RST[S_MU], RST[S_SG], Rbrs], writes=[RPS[bank]])
                vop(lambda E, tk, tt=tt, bank=bank: tk(E.tensor_scalar(
                    pool[:, KST + tt * 512: KST + tt * 512 + 512],
                    ps[bank][:, 0:512], rtm[:, tt:tt + 1], None, op0=ALU.mult)),
                    reads=[RPS[bank], Rrtm],
                    writes=zones(RP, KST + tt * 512, KST + tt * 512 + 512))
            if KDUMP == "l1" and vs == 0:
                for ii in range(4):
                    bdump(256 + ii * 128, pool[:, KST + ii * 1024:KST + (ii + 1) * 1024], 128)
            for sg in range(2):
                def store_v1(E, tk, vs=vs, sg=sg):
                    rv = get_qb(E, sg)
                    tk(E.dma_start(
                        out=shv1[DynSlice(rv, 512), vs * 512:(vs + 1) * 512].rearrange(
                            "(t p) j -> p t j", p=128),
                        in_=pool[:, KST + sg * 2048:KST + (sg + 1) * 2048].rearrange(
                            "p (t j) -> p t j", t=4)))
                gop(store_v1, reads=zones(RP, KST + sg * 2048, KST + (sg + 1) * 2048),
                    writes=[Rshv1], sem="GX")

        barrier([Rshk1, Rshv1])
        attention(1)

    if KSTAGE >= 5:
        layernorm_stats()
        TQ0, TQ1 = 14336, 15360
        for dc in range(NCK):
            tq = TQ0 if dc % 2 == 0 else TQ1
            vop(lambda E, tk, dc=dc, tq=tq: tk(E.tensor_tensor(
                pool[:, tq:tq + 1024], xbuf[:, dc * T:(dc + 1) * T], Rbuf[:, 1024:2048],
                op=ALU.add)), reads=[RX[dc], RRB[1]], writes=zones(RP, tq, tq + 1024))
            for tt in range(8):
                top(lambda E, tk, tq=tq, tt=tt: tk(E.transpose(
                    ps[6][:, (tt % 4) * 128:(tt % 4) * 128 + 128],
                    pool[:, tq + tt * 128:tq + (tt + 1) * 128], identt[:])),
                    reads=zones(RP, tq, tq + 1024) + [Rsmall], writes=[RPS[6]])
                vop(lambda E, tk, tt=tt: tk(E.tensor_reduce(
                    wtm[:, tt:tt + 1], ps[6][:, (tt % 4) * 128:(tt % 4) * 128 + 128],
                    axis=AX.X, op=ALU.max, apply_absolute_value=True)),
                    reads=[RPS[6]], writes=[Rqtm])
            for tt in range(8):
                top(lambda E, tk, tt=tt: tk(E.transpose(
                    ps[7][0:1, (tt % 4) * 128:(tt % 4) * 128 + 128],
                    wtm[:, tt:tt + 1], identt[:])),
                    reads=[Rqtm, Rsmall], writes=[RPS[7]])
                aop(lambda E, tk, tt=tt: tk(E.copy(
                    statr[0:1, S_HD * 1024 + tt * 128:S_HD * 1024 + (tt + 1) * 128],
                    ps[7][0:1, (tt % 4) * 128:(tt % 4) * 128 + 128])),
                    reads=[RPS[7]], writes=[RST[S_HD]])
            vop(lambda E, tk: tk(E.tensor_tensor(
                statr[0:1, S_HD * 1024:S_HD * 1024 + 1024],
                statr[0:1, S_HD * 1024:S_HD * 1024 + 1024], srow(S_RI), op=ALU.mult)),
                reads=[RST[S_RI], RST[S_HD]], writes=[RST[S_HD]])
            vop(lambda E, tk: tk(E.tensor_scalar(
                statr[0:1, S_HD * 1024:S_HD * 1024 + 1024],
                statr[0:1, S_HD * 1024:S_HD * 1024 + 1024], 1e-4, None, op0=ALU.max)),
                reads=[RST[S_HD]], writes=[RST[S_HD]])
            vop(lambda E, tk: tk(E.tensor_scalar(
                dqrows[0:1, 0:1024],
                statr[0:1, S_HD * 1024:S_HD * 1024 + 1024], 1.0 / 224.0, None, op0=ALU.mult)),
                reads=[RST[S_HD]], writes=[Rdq])
            B.op("gpsimd", lambda E, tk, dc=dc: tk(E.dma_start(
                out=dqd[dc:dc + 1, :], in_=dqrows[0:1, 0:1024])),
                reads=[Rdq], writes=[Rdq], sem="GD")
            vop(lambda E, tk: tk(E.reciprocal(
                statr[0:1, S_SG * 1024:S_SG * 1024 + 1024],
                statr[0:1, S_HD * 1024:S_HD * 1024 + 1024])),
                reads=[RST[S_HD]], writes=[RST[S_SG]])
            vop(lambda E, tk: tk(E.tensor_tensor(
                statr[0:1, S_SG * 1024:S_SG * 1024 + 1024],
                statr[0:1, S_SG * 1024:S_SG * 1024 + 1024], srow(S_RI),
                op=ALU.mult)), reads=[RST[S_RI], RST[S_SG]], writes=[RST[S_SG]])
            vop(lambda E, tk: tk(E.tensor_scalar(
                statr[0:1, S_SG * 1024:S_SG * 1024 + 1024],
                statr[0:1, S_SG * 1024:S_SG * 1024 + 1024], 224.0, None,
                op0=ALU.mult)), reads=[RST[S_SG]], writes=[RST[S_SG]])
            bcastf(lambda: Rtmp[:, 0:1024], [RRT],
                   lambda: statr[0:1, S_SG * 1024:S_SG * 1024 + 1024], [RST[S_SG]], 1024, 6)
            vop(lambda E, tk, dc=dc, tq=tq: tk(E.tensor_tensor(
                f8a[:, dc * T:(dc + 1) * T], pool[:, tq:tq + 1024], Rtmp[:, 0:1024],
                op=ALU.mult)), reads=zones(RP, tq, tq + 1024) + [RRT],
                writes=[RF8[dc // 4]])
        if KDUMP == "f1":
            bdump(1024, statr[0:1, S_HD * 1024:S_HD * 1024 + 1024], 1)
            bdump(1025, statr[0:1, S_SG * 1024:S_SG * 1024 + 1024], 1)
            bdump(1026, dqrows[0:1, 0:1024], 1)
            bdump(1027, statr[0:1, S_RI * 1024:S_RI * 1024 + 1024], 1)
        XD = 0
        WSL = [8192, 10240]
        WD = 12288
        for g in range(2):
            for dc in range(NCK):
                p2 = dc % 2
                RDQ = Rdq if p2 == 0 else Rdq2
                RRTp = RRT if p2 == 0 else Rcbc2
                B.op("sync", lambda E, tk, dc=dc, g=g, p2=p2: tk(E.dma_start(
                    out=dqrows[0:1, p2 * 512:(p2 + 1) * 512],
                    in_=dqd[dc:dc + 1, g * 512:g * 512 + 512])),
                    reads=[RDQ], writes=[RDQ], sem="LD")
                bcastf(lambda p2=p2: Rtmp[:, p2 * 512:(p2 + 1) * 512], [RRTp],
                       lambda p2=p2: dqrows[0:1, p2 * 512:(p2 + 1) * 512],
                       [RDQ], 512, 6 + p2)
                vop(lambda E, tk, dc=dc, g=g, p2=p2, RRTp=RRTp: tk(E.tensor_tensor(
                    pool[:, XD + dc * 512: XD + (dc + 1) * 512],
                    f8a[:, dc * T + g * 512: dc * T + g * 512 + 512],
                    Rtmp[:, p2 * 512:(p2 + 1) * 512],
                    op=ALU.mult)), reads=[RF8[dc // 4], RRTp],
                    writes=zones(RP, XD + dc * 512, XD + (dc + 1) * 512))
            def ldW(sl16):
                p = sl16 % 2
                wb = WSL[p]
                ld(pool[:, wb:wb + 2048], l1_wT[sl16], zones(RP, wb, wb + 2048))

            def stageA2(sl16):
                p = sl16 % 2
                wb = WSL[p]
                wz = zones(RP, wb, wb + 2048)
                RW = Rwtm if p == 0 else Rwtm2
                for dc in range(NCK):
                    vop(lambda E, tk, wb=wb, dc=dc, p=p: tk(E.tensor_reduce(
                        wtm[:, p * 16 + dc:p * 16 + dc + 1],
                        pool[:, wb + dc * 128:wb + (dc + 1) * 128],
                        axis=AX.X, op=ALU.max, apply_absolute_value=True)),
                        reads=wz, writes=[RW])

            def stageA3(sl16):
                p = sl16 % 2
                RW = Rwtm if p == 0 else Rwtm2
                RWC_ = RST[S_WC] if p == 0 else Rwc1
                RCB = Rcbc if p == 0 else Rcbc2
                wcb = S_WC * 1024 + p * 96
                cb0 = 16 + p * 48
                pb = 6 + p
                top(lambda E, tk, p=p, pb=pb: tk(E.transpose(
                    ps[pb][0:16, 0:128], wtm[:, p * 16:(p + 1) * 16], identt[:])),
                    reads=[RW, Rsmall], writes=[RPS[pb]])
                vop(lambda E, tk, p=p, pb=pb: tk(E.tensor_reduce(
                    wtm2[:, p:p + 1], ps[pb][0:16, 0:128], axis=AX.X, op=ALU.max)),
                    reads=[RPS[pb]], writes=[RW])
                top(lambda E, tk, p=p, pb=pb: tk(E.transpose(
                    ps[pb][0:1, 128:144], wtm2[:, p:p + 1], identt[0:16, 0:16])),
                    reads=[RW, Rsmall], writes=[RPS[pb]])
                aop(lambda E, tk, wcb=wcb, pb=pb: tk(E.copy(
                    statr[0:1, wcb:wcb + 16], ps[pb][0:1, 128:144])),
                    reads=[RPS[pb]], writes=[RWC_])
                vop(lambda E, tk, wcb=wcb: tk(E.tensor_scalar(
                    statr[0:1, wcb:wcb + 16],
                    statr[0:1, wcb:wcb + 16], 1e-4, None, op0=ALU.max)),
                    reads=[RWC_], writes=[RWC_])
                vop(lambda E, tk, wcb=wcb: tk(E.reciprocal(
                    statr[0:1, wcb + 16:wcb + 32], statr[0:1, wcb:wcb + 16])),
                    reads=[RWC_], writes=[RWC_])
                vop(lambda E, tk, wcb=wcb: tk(E.tensor_scalar(
                    statr[0:1, wcb + 16:wcb + 32],
                    statr[0:1, wcb + 16:wcb + 32], 224.0, None, op0=ALU.mult)),
                    reads=[RWC_], writes=[RWC_])
                vop(lambda E, tk, wcb=wcb: tk(E.tensor_scalar(
                    statr[0:1, wcb + 32:wcb + 48],
                    statr[0:1, wcb:wcb + 16], 1.0 / 224.0, None, op0=ALU.mult)),
                    reads=[RWC_], writes=[RWC_])
                bcastf(lambda cb0=cb0: cbc[:, cb0:cb0 + 16], [RCB],
                       lambda wcb=wcb: statr[0:1, wcb + 16:wcb + 32], [RWC_], 16, pb)
                bcastf(lambda cb0=cb0: cbc[:, cb0 + 16:cb0 + 32], [RCB],
                       lambda wcb=wcb: statr[0:1, wcb + 32:wcb + 48], [RWC_], 16, pb)

            def stageBpre(sl16):
                p = sl16 % 2
                wb = WSL[p]
                wz = zones(RP, wb, wb + 2048)
                WDp = WD if p == 0 else 14336
                RCB = Rcbc if p == 0 else Rcbc2
                cb0 = 16 + p * 48
                for dc in range(NCK):
                    f8r = p * 2 + dc % 2
                    vop(lambda E, tk, wb=wb, dc=dc, f8r=f8r, cb0=cb0: tk(E.tensor_scalar(
                        f8b[:, f8r * 128:(f8r + 1) * 128],
                        pool[:, wb + dc * 128: wb + (dc + 1) * 128],
                        cbc[:, cb0 + dc:cb0 + dc + 1], None, op0=ALU.mult)),
                        reads=wz + [RCB], writes=[RF8B[p]])
                    vop(lambda E, tk, dc=dc, f8r=f8r, cb0=cb0, WDp=WDp: tk(E.tensor_scalar(
                        pool[:, WDp + dc * 128: WDp + (dc + 1) * 128],
                        f8b[:, f8r * 128:(f8r + 1) * 128],
                        cbc[:, cb0 + 16 + dc:cb0 + 17 + dc], None, op0=ALU.mult)),
                        reads=[RF8B[p], RCB],
                        writes=zones(RP, WDp + dc * 128, WDp + (dc + 1) * 128))

            def stageBmm(sl16):
                p = sl16 % 2
                WDp = WD if p == 0 else 14336
                wdz = zones(RP, WDp, WDp + 2048)
                bank = p
                def emitW(E, tk, bank=bank, WDp=WDp):
                    for dc in range(NCK):
                        ins = E.matmul(ps[bank][:, 0:512],
                                       pool[:, WDp + dc * 128: WDp + dc * 128 + 128],
                                       pool[:, XD + dc * 512: XD + dc * 512 + 512],
                                       start=(dc == 0), stop=(dc == NCK - 1))
                    tk(ins)
                top(emitW, reads=wdz + zones(RP, XD, XD + 8192), writes=[RPS[bank]])

            def stageBevac(sl16, g):
                p = sl16 % 2
                bank = p
                ROB = Robf if p == 0 else Robf2
                aop(lambda E, tk, bank=bank, p=p: tk(E.copy(
                    obf[:, p * 512:(p + 1) * 512], ps[bank][:, 0:512])),
                    reads=[RPS[bank]], writes=[ROB])
                vop(lambda E, tk, sl16=sl16, g=g, p=p: tk(E.tensor_tensor(
                    xbuf[:, sl16 * T + g * 512: sl16 * T + g * 512 + 512],
                    xbuf[:, sl16 * T + g * 512: sl16 * T + g * 512 + 512],
                    obf[:, p * 512:(p + 1) * 512],
                    op=ALU.add)), reads=[RX[sl16], ROB], writes=[RX[sl16]])

            ldW(0)
            stageA2(0)
            stageA3(0)
            ldW(1)
            stageA2(1)
            stageBpre(0)
            for sl16 in range(16):
                if sl16 + 1 < 16:
                    stageA3(sl16 + 1)
                    stageBpre(sl16 + 1)
                stageBmm(sl16)
                if sl16 + 2 < 16:
                    ldW(sl16 + 2)
                    stageA2(sl16 + 2)
                stageBevac(sl16, g)

    if not KDUMP:
        for dc in range(NCK):
            B.op("gpsimd", lambda E, tk, dc=dc: tk(E.dma_start(
                out=out[dc * 128:(dc + 1) * 128, :], in_=xbuf[:, dc * T:(dc + 1) * T])),
                reads=[RX[dc]], writes=(), sem="GD")
    for q in ("GD", "GX"):
        for i in range(NQ):
            s = B.sems[q + str(i)]
            if s.c > 0:
                B.plan["gpsimd"].append(lambda E, s=s, v=s.c: E.wait_ge(s.h, v))

    with ExitStack() as ses:
        for n, s in B.sems.items():
            s.h = ses.enter_context(nc.semaphore(n))
        with nc.Block() as block:
            @block.sync
            def _(E):
                for f in B.plan["sync"]:
                    f(E)

            @block.tensor
            def _(E):
                for f in B.plan["tensor"]:
                    f(E)

            @block.vector
            def _(E):
                for f in B.plan["vector"]:
                    f(E)

            @block.scalar
            def _(E):
                for f in B.plan["scalar"]:
                    f(E)

            @block.gpsimd
            def _(E):
                for f in B.plan["gpsimd"]:
                    f(E)
    es.close()
    _cache["nc"] = nc
    return nc


def _slabify(W, ncols):
    F = W.shape[1]
    nsl = F // ncols
    return np.ascontiguousarray(
        W.reshape(16, 128, nsl, ncols).transpose(2, 1, 0, 3).reshape(nsl, 128, 16 * ncols),
        dtype=np.float32)


def prepare(inputs):
    f32 = lambda a: np.ascontiguousarray(a, dtype=np.float32)

    qT = _slabify(f32(inputs["L0_attn_q_proj"]).T, 256)
    kT = _slabify(f32(inputs["L0_attn_k_proj"]).T, 256)
    vT = _slabify(f32(inputs["L0_attn_v_proj"]).T, 256)
    oT = _slabify(f32(inputs["L0_attn_o_proj"]).T, 256)
    w2 = _slabify(f32(inputs["L0_ffn_input2"]), 256)
    qn = f32(inputs["L0_attn_q_norm"]).reshape(2, 128).T.copy()
    kn = f32(inputs["L0_attn_k_norm"]).reshape(2, 128).T.copy()
    sc = f32(inputs["L0_ffn_scale"]).reshape(1, 1)
    sc1 = f32(inputs["L0_ffn_scale1"]).reshape(1, 1)
    w1q = _slabify(f32(inputs["L1_attn_Wq"]).transpose(1, 0, 2).reshape(D, D), 128)
    w1k = _slabify(f32(inputs["L1_attn_Wk"]).transpose(1, 0, 2).reshape(D, D), 128)
    w1v = _slabify(f32(inputs["L1_attn_Wv"]).transpose(1, 0, 2).reshape(D, D), 512)
    w1o = _slabify(f32(inputs["L1_attn_Wo"]), 256)
    w1f = _slabify(f32(inputs["L1_ffn_weight"]).T, 128)
    bq = f32(inputs["L1_attn_bq"]).reshape(1, D)
    bk = f32(inputs["L1_attn_bk"]).reshape(1, D)
    bv = f32(inputs["L1_attn_bv"]).reshape(1, D)
    bo = f32(inputs["L1_attn_bo"]).reshape(16, 128).T.copy()
    ident = np.eye(128, dtype=np.float32)

    x = f32(inputs["L0_attn_x"])
    in_maps = []
    for c in range(8):
        b, s = c // 2, c % 2
        qb = (512 * s, 512 * s + 1024)
        xs = np.ascontiguousarray(np.concatenate(
            [x[b, qb[0]:qb[0] + 512, :], x[b, qb[1]:qb[1] + 512, :]], 0).T)
        pos = np.concatenate(
            [qb[0] + np.arange(512), qb[1] + np.arange(512)]).astype(np.float64)
        inv = (1.0 / (1e6 ** (np.arange(128, dtype=np.float64) / 128.0)))
        fr = pos[None, :] * inv[:, None]
        cosv = np.cos(fr).astype(np.float32)
        sinv = np.sin(fr).astype(np.float32)
        crow = np.zeros((1, NSLOT_MAX), dtype=np.float32)
        for i, (kind, g, kt) in enumerate(SLOTS):
            crow[0, i] = 128.0 * kt - qb[g]
        in_maps.append({
            "x_in": xs, "l0_qT": qT, "l0_kT": kT, "l0_vT": vT, "l0_oT": oT,
            "l0_qn": qn, "l0_kn": kn, "l0_w2": w2, "l0_sc": sc, "l0_sc1": sc1,
            "l1_qT": w1q, "l1_kT": w1k, "l1_vT": w1v, "l1_wo": w1o,
            "l1_bqr": bq, "l1_bkr": bk, "l1_bvr": bv, "l1_bo": bo, "l1_wT": w1f,
            "cosT": cosv, "sinT": sinv, "crowI": crow, "identI": ident,
        })
    return in_maps


def kernel(**inputs):
    nc = build_program()
    in_maps = prepare(inputs)
    res = run_bass_kernel_spmd(nc, in_maps, core_ids=list(range(8)))
    outp = np.empty((4, 2048, 2048), dtype=np.float32)
    for c in range(8):
        b, s = c // 2, c % 2
        qb = (512 * s, 512 * s + 1024)
        ot = res.results[c]["o"].T
        outp[b, qb[0]:qb[0] + 512, :] = ot[0:512]
        outp[b, qb[1]:qb[1] + 512, :] = ot[512:1024]
    return outp


if __name__ == "__main__":
    nc = build_program()
    print("built; slots:", SLOTS)
    for e in ENGS:
        pass
